# version 26
# baseline (speedup 1.0000x reference)
import contextlib
import numpy as np
import ml_dtypes
import concourse.bass as bass
import concourse.mybir as mybir
import bass_rust
from concourse.bass_utils import run_bass_kernel_spmd

F32 = mybir.dt.float32
BF16 = mybir.dt.bfloat16
AF = mybir.ActivationFunctionType
ALU = mybir.AluOpType
AX = mybir.AxisListType.X

D = 2048
T = 1024
NCH = 16
HP = 16
KC = 32
GROUPS = [[0, 1, 2, 3], [4, 5, 6, 7]]
SHIFT = 20.0
EPS = 1e-5
BIG = 1e30
HREV = 5632

ENGINES = ("sync", "tensor", "vector", "scalar", "gpsimd")


class Sem:
    def __init__(self, handle):
        self.h = handle
        self.n = 0

    def inc(self, k=1):
        self.n += k
        return self.n


class B:

    def __init__(self):
        self.streams = {e: [] for e in ENGINES}
        self.autosem = {}

    def emit(self, eng, fn, inc=None, k=1, raw=False):
        a = None if raw else self.autosem.get(eng)
        if a is not None:
            assert inc is None or inc is a, "vector/scalar ops must inc engine sem"
            self.wait(eng, a, a.n)
            inc = a
            k = 1
        if inc is not None:
            inc.inc(k)
            self.streams[eng].append(lambda e, f=fn, s=inc.h, kk=k: f(e).then_inc(s, kk))
        else:
            self.streams[eng].append(fn)

    def wait(self, eng, sem, val):
        if val <= 0:
            return
        self.streams[eng].append(lambda e, s=sem.h, v=val: e.wait_ge(s, v))


def hap(base_ap, offset, dims):
    c = base_ap.copy()
    c.ap = bass_rust.VecI64Pair(list(dims))
    c.offset = offset
    return c


def build_nc(dbg=False):
    nc = bass.Bass("TRN2", target_bir_lowering=False)

    xin = nc.dram_tensor("xin", [D, T], F32, kind="ExternalInput").ap()
    wqT = nc.dram_tensor("wqT", [D, D], F32, kind="ExternalInput").ap()
    wkT = nc.dram_tensor("wkT", [D, D], F32, kind="ExternalInput").ap()
    wvT = nc.dram_tensor("wvT", [D, D], F32, kind="ExternalInput").ap()
    woT = nc.dram_tensor("woT", [D, D], F32, kind="ExternalInput").ap()
    wfT = nc.dram_tensor("wfT", [D, D], F32, kind="ExternalInput").ap()
    atT = nc.dram_tensor("atT", [D, 128], F32, kind="ExternalInput").ap()
    bff = nc.dram_tensor("bff", [128, D], F32, kind="ExternalInput").ap()
    rtT = nc.dram_tensor("rtT", [D, 8], F32, kind="ExternalInput").ap()
    relw = nc.dram_tensor("relw", [HREV, 64], F32, kind="ExternalInput").ap()
    jmat = nc.dram_tensor("jmat", [128, 128], F32, kind="ExternalInput").ap()
    idm = nc.dram_tensor("idm", [128, 128], F32, kind="ExternalInput").ap()
    e8m = nc.dram_tensor("e8m", [8, 128], F32, kind="ExternalInput").ap()

    xout = nc.dram_tensor("xout", [D, T], F32, kind="ExternalOutput").ap()

    x1d = nc.dram_tensor("x1d", [D, T], F32).ap()
    wfsh = nc.dram_tensor("wfsh", [D, D], BF16).ap()
    wosh = nc.dram_tensor("wosh", [D, D], BF16).ap()
    x2d = nc.dram_tensor("x2d", [D, T], F32).ap()
    qtd = nc.dram_tensor("qtd", [D, T], BF16).ap()
    ctxd = nc.dram_tensor("ctxd", [D, T], BF16).ap()
    hrevd = nc.dram_tensor("hrevd", [1, HREV], BF16).ap()
    kbs = [nc.dram_tensor(f"kb{s}", [512, T], BF16) for s in range(4)]
    kgs = [nc.dram_tensor(f"kg{s}", [4 * 512, T], BF16) for s in range(4)]
    vbs = [nc.dram_tensor(f"vb{s}", [T, 520], BF16) for s in range(4)]
    vgs = [nc.dram_tensor(f"vg{s}", [4 * T, 520], BF16) for s in range(4)]

    dbg_out = {}
    if dbg:
        for nm, shp, dt in [("d_x1", [D, T], F32), ("d_ln2", [D, T], BF16),
                            ("d_qt", [D, T], BF16), ("d_ebt", [128, T], BF16),
                            ("d_ctx", [D, T], BF16), ("d_x2", [D, T], F32),
                            ("d_lg", [128, 8], F32), ("d_w", [128, 8], BF16),
                            ("d_z", [128, T], BF16), ("d_hrev", [1, HREV], BF16)]:
            dbg_out[nm] = nc.dram_tensor(nm, shp, dt, kind="ExternalOutput").ap()

    b = B()
    ctx0 = contextlib.ExitStack()
    E = ctx0.enter_context

    names = ["pool", "pe", "act", "dve", "st", "cc", "bar", "init", "misc",
             "rel0", "rel1", "g0", "g1", "xs0", "xs1", "xs2", "w0", "w1",
             "qkvw0", "qkvw1", "att0", "att1", "ldctx", "so0", "so1", "qk", "ctxst",
             "wh0", "wh1", "gx0", "gx1", "gx2", "gso0", "gso1", "gld",
             "wsh", "fsh"]
    sem = {n: Sem(E(nc.semaphore(n))) for n in names}
    s_pool, s_pe, s_act, s_dve, s_st, s_cc, s_bar, s_init = (
        sem[n] for n in ["pool", "pe", "act", "dve", "st", "cc", "bar", "init"])
    s_misc = sem["misc"]
    s_rel = [sem["rel0"], sem["rel1"]]
    s_g = [sem["g0"], sem["g1"]]
    s_xs = [sem["xs0"], sem["xs1"], sem["xs2"]]
    s_w = [sem["w0"], sem["w1"]]
    s_qkvw = [sem["qkvw0"], sem["qkvw1"]]
    s_att = [sem["att0"], sem["att1"]]
    s_ldctx = sem["ldctx"]
    s_so = [sem["so0"], sem["so1"]]
    s_qk = sem["qk"]
    s_ctxst = sem["ctxst"]
    s_wh = [sem["wh0"], sem["wh1"]]
    s_gx = [sem["gx0"], sem["gx1"], sem["gx2"]]
    s_gso = [sem["gso0"], sem["gso1"]]
    s_gld = sem["gld"]
    s_wsh = sem["wsh"]
    s_fsh = sem["fsh"]
    b.autosem["vector"] = s_dve
    b.autosem["scalar"] = s_act

    lnT = [E(nc.sbuf_tensor(f"lnT{i}", [128, T], BF16)) for i in range(NCH)]
    s_ones128 = E(nc.sbuf_tensor("ones128", [128, 1], BF16))
    s_onesF = E(nc.sbuf_tensor("onesF", [128, 128], F32))
    s_J = E(nc.sbuf_tensor("sJ", [128, 128], BF16))
    s_idb = E(nc.sbuf_tensor("sidb", [128, 128], BF16))
    s_idf = E(nc.sbuf_tensor("sidf", [128, 128], F32))
    s_E8 = E(nc.sbuf_tensor("sE8", [8, 128], BF16))
    s_AT = [E(nc.sbuf_tensor(f"sAT{i}", [128, 128], BF16)) for i in range(NCH)]
    s_RT = [E(nc.sbuf_tensor(f"sRT{i}", [128, 8], BF16)) for i in range(NCH)]
    s_Bf = E(nc.sbuf_tensor("sBf", [128, D], BF16))
    s_wsl = [E(nc.sbuf_tensor(f"wsl{i}", [128, D], BF16)) for i in range(2)]
    s_x = None
    s_xo = None
    s_wf32 = None
    s_cst = E(nc.sbuf_tensor("scst", [128, 8], F32))
    s_rows = E(nc.sbuf_tensor("srows", [1, 4096], F32))
    s_r8 = E(nc.sbuf_tensor("sr8", [128, 16], F32))
    s_lg = E(nc.sbuf_tensor("slg", [128, 8], F32))
    s_lgb = [E(nc.sbuf_tensor(f"slgb{i}", [128, 8], F32)) for i in range(3)]
    s_w8 = E(nc.sbuf_tensor("sw8", [128, 8], BF16))
    ebt_scope = contextlib.ExitStack()
    ebt = [ebt_scope.enter_context(nc.sbuf_tensor(f"ebt{i}", [128, T], BF16))
           for i in range(KC)]

    CONSTS = [-SHIFT, 1.0 / 2048.0, EPS, BIG, -1.0, 0.5]
    for j, v in enumerate(CONSTS):
        nc.const_aps.register(F32, float(v), s_cst[:, j:j + 1])

    p_all = E(nc.psum_tensor("pall", [128, 4096], F32))

    pe_mark = {}
    phase_stack = [None]

    def open_phase():
        phase_stack[0] = contextlib.ExitStack()

    def close_phase():
        phase_stack[0].close()
        phase_stack[0] = None

    def pa(name, shape, dtype):
        return phase_stack[0].enter_context(nc.sbuf_tensor(name, shape, dtype))

    def barrier():
        tgt = s_bar.n + 5
        s_bar.n = tgt
        for eng in ENGINES:
            b.emit(eng, lambda e: e.sem_inc(s_bar.h, 1), raw=True)
        for eng in ENGINES:
            b.wait(eng, s_bar, tgt)
        for eng in ENGINES:
            for s2 in (s_pe, s_act, s_dve):
                b.wait(eng, s2, s2.n)
        for eng in ("sync", "gpsimd"):
            for ss in (s_st, s_so[0], s_so[1], s_qk, s_ctxst):
                b.wait(eng, ss, ss.n)

    b.emit("vector", lambda e: e.memset(s_ones128[:], 1.0))
    b.emit("vector", lambda e: e.memset(s_onesF[:], 1.0))
    for j, v in enumerate(CONSTS):
        b.emit("vector", lambda e, j=j, v=v: e.memset(s_cst[:, j:j + 1], float(v)))
    b.emit("vector", lambda e: e.memset(s_cst[:, 6:8], 0.0))
    init_done = s_dve.n
    for eng in ENGINES:
        if eng != "vector":
            b.wait(eng, s_dve, init_done)
    b.emit("gpsimd", lambda e: e.dma_start(out=s_J[:], in_=jmat), inc=s_pool, k=16)
    b.emit("gpsimd", lambda e: e.dma_start(out=s_idb[:], in_=idm), inc=s_pool, k=16)
    b.emit("gpsimd", lambda e: e.dma_start(out=s_E8[:], in_=e8m), inc=s_pool, k=16)
    b.emit("gpsimd", lambda e: e.dma_start(out=s_Bf[:], in_=bff), inc=s_pool, k=16)
    for i in range(NCH):
        b.emit("gpsimd", lambda e, i=i: e.dma_start(
            out=s_AT[i][:], in_=atT[128 * i:128 * i + 128, :]), inc=s_pool, k=16)
        b.emit("gpsimd", lambda e, i=i: e.dma_start(
            out=s_RT[i][:], in_=rtT[128 * i:128 * i + 128, :]), inc=s_pool, k=16)
    pool_init = s_pool.n
    b.emit("sync", lambda e: e.dma_start(out=s_idf[:], in_=idm), inc=s_misc, k=16)
    idf_ld = s_misc.n

    open_phase()
    nch0 = HREV // 128
    s_relall = pa("p0rel", [128, nch0 * 64], F32)
    s_rs = pa("p0rs", [128, nch0], F32)
    s_hrow = pa("p0hrow", [44, 128], BF16)
    b.emit("sync", lambda e: e.dma_start(
        out=s_relall[:],
        in_=hap(relw, 0, [(64, 128), (128 * 64, nch0), (1, 64)])), inc=s_rel[0], k=16)
    b.wait("vector", s_rel[0], s_rel[0].n)
    for i in range(nch0):
        b.emit("vector", lambda e, i=i: e.reduce_sum(
            s_rs[:, i:i + 1], s_relall[:, 64 * i:64 * i + 64], axis=AX))
    b.wait("tensor", s_dve, s_dve.n)
    b.wait("tensor", s_misc, idf_ld)
    b.emit("tensor", lambda e: e.transpose(
        p_all[0:nch0, 3584:3712], s_rs[:], s_idf[:]), inc=s_pe)
    b.wait("scalar", s_pe, s_pe.n)
    b.emit("scalar", lambda e: e.activation(
        s_hrow[:], p_all[0:nch0, 3584:3712], AF.Exp, scale=1.0 / 64.0))
    b.wait("sync", s_act, s_act.n)
    b.emit("sync", lambda e: e.dma_start(
        out=hap(hrevd, 0, [(128, 44), (1, 128)]), in_=s_hrow[:]), inc=s_st, k=16)
    if dbg:
        b.emit("sync", lambda e: e.dma_start(
            out=hap(dbg_out["d_hrev"], 0, [(128, 44), (1, 128)]), in_=s_hrow[:]),
            inc=s_st, k=16)
    hrev_st = s_st.n
    s_G = [pa(f"p0G{i}", [128, T], BF16) for i in range(2)]
    for kc in range(KC):
        bb = kc % 2
        qeng = "sync" if bb == 0 else "scalar"
        off = 128 * (31 - kc)
        b.wait(qeng, s_st, hrev_st)
        b.wait(qeng, s_pe, pe_mark.get(("p0G", kc - 2), 0))
        b.emit(qeng, lambda e, off=off, bb=bb: e.dma_start(
            out=s_G[bb][:], in_=hap(hrevd, off, [(1, 128), (1, T)])),
            inc=s_g[bb], k=16, raw=True)
        b.wait("tensor", s_g[bb], s_g[bb].n)
        b.wait("tensor", s_pool, pool_init)
        b.wait("tensor", s_dve, pe_mark.get(("p0ebv", kc - 1), 0))
        for h in range(2):
            b.emit("tensor", lambda e, bb=bb, h=h: e.matmul(
                p_all[0:128, 512 * (5 + h):512 * (5 + h) + 512], s_J[:],
                s_G[bb][:, 512 * h:512 * h + 512], start=True, stop=True), inc=s_pe)
        pe_mark[("p0G", kc)] = s_pe.n
        b.wait("vector", s_pe, s_pe.n)
        b.emit("vector", lambda e, kc=kc: e.tensor_copy(
            ebt[kc][:], p_all[0:128, 2560:3584]))
        pe_mark[("p0ebv", kc)] = s_dve.n
    if dbg:
        b.wait("sync", s_dve, s_dve.n)
        b.emit("sync", lambda e: e.dma_start(out=dbg_out["d_ebt"], in_=ebt[4][:]),
               inc=s_st, k=16)
    barrier()
    close_phase()

    def layer_norm(tag, src_dram, xb_tiles, s_sq, s_tmp):
        for c in range(NCH):
            xbuf = c % 3
            b.wait("sync", s_dve, pe_mark.get((tag, "cast", c - 3), 0))
            b.emit("sync", lambda e, c=c, t=s_x[xbuf]: e.dma_start(
                out=t[:], in_=src_dram[128 * c:128 * c + 128, :]),
                inc=s_xs[xbuf], k=16)
            b.wait("vector", s_xs[xbuf], s_xs[xbuf].n)
            b.emit("vector", lambda e, c=c, t=s_x[xbuf]: e.tensor_copy(
                xb_tiles[c][:], t[:]))
            pe_mark[(tag, "cast", c)] = s_dve.n
            sqb = c % 2
            b.wait("vector", s_pe, pe_mark.get((tag, "sqmm", c - 2), 0))
            b.emit("vector", lambda e, c=c, sqb=sqb: e.tensor_tensor(
                s_sq[sqb][:], xb_tiles[c][:], xb_tiles[c][:], ALU.mult))
            b.wait("tensor", s_dve, s_dve.n)
            for h in range(2):
                b.emit("tensor", lambda e, c=c, h=h: e.matmul(
                    p_all[0:1, 512 * h:512 * h + 512], s_ones128[:],
                    xb_tiles[c][:, 512 * h:512 * h + 512],
                    start=(c == 0), stop=(c == NCH - 1)), inc=s_pe)
            for h in range(2):
                b.emit("tensor", lambda e, h=h, sqb=sqb, c=c: e.matmul(
                    p_all[0:1, 1024 + 512 * h:1536 + 512 * h], s_ones128[:],
                    s_sq[sqb][:, 512 * h:512 * h + 512],
                    start=(c == 0), stop=(c == NCH - 1)), inc=s_pe)
            pe_mark[(tag, "sqmm", c)] = s_pe.n
        b.wait("vector", s_pe, s_pe.n)

        M = s_rows[0:1, 0:1024]
        E2 = s_rows[0:1, 1024:2048]
        T3 = s_rows[0:1, 2048:3072]
        V4 = s_rows[0:1, 3072:4096]

        b.emit("vector", lambda e: e.tensor_scalar(
            M, p_all[0:1, 0:1024], 1.0 / 2048.0, None, ALU.mult))
        b.emit("vector", lambda e: e.tensor_scalar(
            E2, p_all[0:1, 1024:2048], 1.0 / 2048.0, None, ALU.mult))
        b.emit("vector", lambda e: e.tensor_tensor(T3, M, M, ALU.mult))
        b.emit("vector", lambda e: e.tensor_tensor(V4, E2, T3, ALU.subtract))
        b.wait("scalar", s_dve, s_dve.n)

        b.emit("scalar", lambda e: e.activation(E2, V4, AF.Ln, bias=EPS, scale=1.0))
        b.emit("scalar", lambda e: e.activation(T3, E2, AF.Exp, scale=-0.5))
        b.wait("vector", s_act, s_act.n)
        b.emit("vector", lambda e: e.tensor_tensor(V4, M, T3, ALU.mult))
        b.wait("tensor", s_dve, s_dve.n)
        for h in range(2):
            b.emit("tensor", lambda e, h=h: e.matmul(
                p_all[0:128, 512 * h:512 * h + 512], s_onesF[0:1, :],
                s_rows[0:1, 2048 + 512 * h:2048 + 512 * h + 512],
                start=True, stop=True), inc=s_pe)
            b.emit("tensor", lambda e, h=h: e.matmul(
                p_all[0:128, 1024 + 512 * h:1536 + 512 * h], s_onesF[0:1, :],
                s_rows[0:1, 3072 + 512 * h:3072 + 512 * h + 512],
                start=True, stop=True), inc=s_pe)
        pe_exp = s_pe.n
        for c in range(NCH):
            tb = c % 2
            b.wait("vector", s_pe, pe_exp)
            b.emit("vector", lambda e, c=c, tb=tb: e.tensor_tensor(
                s_tmp[tb][:], xb_tiles[c][:], p_all[0:128, 1024:2048], ALU.subtract),
    )
            b.emit("vector", lambda e, c=c, tb=tb: e.tensor_tensor(
                lnT[c][:], s_tmp[tb][:], p_all[0:128, 0:1024], ALU.mult))
        return s_dve.n

    def ffn(tag, src_dram, dst_dram, wT_dram, apply_mark, fs, post_oc=None,
            w_guard=0, wf_guard=0, sync_w=False):
        s_lgT, s_wT8, s_wexp, s_Z = fs
        b.wait("tensor", s_dve, apply_mark)
        for c in range(NCH):
            b.emit("tensor", lambda e, c=c: e.matmul(
                p_all[0:128, 0:512], s_AT[c][:], lnT[c][:, 0:512],
                start=(c == 0), stop=(c == NCH - 1)), inc=s_pe)
            b.emit("tensor", lambda e, c=c: e.matmul(
                p_all[0:128, 512:1024], s_AT[c][:], lnT[c][:, 512:1024],
                start=(c == 0), stop=(c == NCH - 1)), inc=s_pe)
            b.emit("tensor", lambda e, c=c: e.matmul(
                p_all[0:8, 1024:1536], s_RT[c][:], lnT[c][:, 0:512],
                start=(c == 0), stop=(c == NCH - 1)), inc=s_pe)
            b.emit("tensor", lambda e, c=c: e.matmul(
                p_all[0:8, 1536:2048], s_RT[c][:], lnT[c][:, 512:1024],
                start=(c == 0), stop=(c == NCH - 1)), inc=s_pe)
        pe_lora = s_pe.n
        b.wait("vector", s_pe, pe_lora)
        b.emit("vector", lambda e: e.tensor_copy(s_lgT[:], p_all[0:8, 1024:2048]),
   )
        lgT_ev = s_dve.n
        for c in range(8):
            b.wait("tensor", s_dve, max(lgT_ev, pe_mark.get((tag, "lgdone", c - 4), 0)))
            b.emit("tensor", lambda e, c=c: e.transpose(
                p_all[0:128, 2048 + 8 * (c % 4):2048 + 8 * (c % 4) + 8],
                s_lgT[0:8, 128 * c:128 * c + 128], s_idf[0:8, 0:8]), inc=s_pe)
            b.wait("vector", s_pe, s_pe.n)
            col = 2048 + 8 * (c % 4)

            b.emit("vector", lambda e, col=col: e.tensor_copy(
                s_lg[:], p_all[0:128, col:col + 8]))
            b.emit("vector", lambda e: e.reduce_max(s_r8[:, 0:1], s_lg[:], axis=AX))
            b.emit("vector", lambda e: e.tensor_scalar(
                s_lgb[0][:], s_lg[:], s_r8[:, 0:1], None, ALU.is_ge))
            b.emit("vector", lambda e: e.tensor_scalar(
                s_lgb[0][:], s_lgb[0][:], BIG, None, ALU.mult))
            b.emit("vector", lambda e: e.tensor_tensor(
                s_lgb[1][:], s_lg[:], s_lgb[0][:], ALU.subtract))
            b.emit("vector", lambda e: e.reduce_max(s_r8[:, 1:2], s_lgb[1][:], axis=AX))
            b.emit("vector", lambda e: e.tensor_scalar(
                s_r8[:, 2:3], s_r8[:, 0:1], -1.0, None, ALU.mult))
            pe_mark[(tag, "lgdone", c)] = s_dve.n
            b.wait("scalar", s_dve, s_dve.n)
            b.emit("scalar", lambda e: e.activation(
                s_lgb[2][:], s_lg[:], AF.Exp, bias=s_r8[:, 2:3], scale=1.0))
            b.wait("vector", s_act, s_act.n)
            b.wait("vector", s_pe, pe_mark.get((tag, "wtp", c - 1), 0))

            b.emit("vector", lambda e: e.tensor_scalar(
                s_lgb[0][:], s_lg[:], s_r8[:, 1:2], None, ALU.is_ge))
            b.emit("vector", lambda e: e.tensor_tensor(
                s_lgb[1][:], s_lgb[2][:], s_lgb[0][:], ALU.mult))
            b.emit("vector", lambda e: e.reduce_sum(s_r8[:, 3:4], s_lgb[1][:], axis=AX))
            b.emit("vector", lambda e: e.reciprocal(s_r8[:, 4:5], s_r8[:, 3:4]))
            b.emit("vector", lambda e: e.tensor_scalar(
                s_w8[:], s_lgb[1][:], s_r8[:, 4:5], None, ALU.mult))
            if dbg and c == 0 and tag == "f1":
                b.wait("sync", s_dve, s_dve.n)
                b.emit("sync", lambda e: e.dma_start(out=dbg_out["d_lg"], in_=s_lg[:]),
                       inc=s_st, k=16)
                b.emit("gpsimd", lambda e: e.dma_start(out=dbg_out["d_w"], in_=s_w8[:]),
                       inc=s_st, k=16)
            b.wait("tensor", s_dve, s_dve.n)
            b.emit("tensor", lambda e: e.transpose(
                p_all[0:8, 2560:2624].bitcast(BF16), s_w8[:], s_idb[:]), inc=s_pe)
            pe_mark[(tag, "wtp", c)] = s_pe.n
            b.wait("vector", s_pe, s_pe.n)
            b.emit("vector", lambda e, c=c: e.tensor_copy(
                s_wT8[0:8, 128 * c:128 * c + 128], p_all[0:8, 2560:2624].bitcast(BF16)),
    )
        b.wait("tensor", s_dve, s_dve.n)
        for h in range(2):
            b.emit("tensor", lambda e, h=h: e.matmul(
                p_all[0:128, 3072 + 512 * h:3072 + 512 * h + 512], s_E8[:],
                s_wT8[0:8, 512 * h:512 * h + 512], start=True, stop=True), inc=s_pe)
        b.wait("vector", s_pe, s_pe.n)
        b.emit("vector", lambda e: e.tensor_copy(s_wexp[:], p_all[0:128, 3072:4096]),
   )
        b.emit("vector", lambda e: e.tensor_tensor(
            s_Z[:], p_all[0:128, 0:1024], s_wexp[:], ALU.mult))
        z_done = s_dve.n
        if dbg and tag == "f1":
            b.wait("sync", s_dve, z_done)
            b.emit("sync", lambda e: e.dma_start(out=dbg_out["d_z"], in_=s_Z[:]),
                   inc=s_st, k=16)
        for oc in range(NCH):
            wb = oc % 2
            if sync_w:
                if oc < 2:
                    b.wait("sync", s_fsh, 256)
                b.wait("sync", s_dve, pe_mark.get((tag, "wts", oc - 2), 0))
                b.wait("sync", s_pe, max(w_guard,
                                         pe_mark.get((tag, "base", oc - 2), 0)))
                b.emit("sync", lambda e, oc=oc, wb=wb: e.dma_start(
                    out=s_wsl[wb][:],
                    in_=wfsh[128 * oc:128 * oc + 128, :]),
                    inc=s_wh[wb], k=16)
                b.wait("vector", s_wh[wb], s_wh[wb].n)
                b.emit("vector", lambda e, wb=wb: e.tensor_scalar(
                    s_wsl[wb][:], s_wsl[wb][:], 0.5, None, ALU.mult))
                pe_mark[(tag, "wts", oc)] = s_dve.n
                pair = oc % 3
                b.wait("tensor", s_dve, max(pe_mark[(tag, "wts", oc)], z_done,
                                            pe_mark.get((tag, "evac", oc - 3), 0)))
            else:
                b.wait("gpsimd", s_pe, max(w_guard,
                                            pe_mark.get((tag, "base", oc - 2), 0)))
                b.emit("gpsimd", lambda e, oc=oc, wb=wb: e.dma_start(
                    out=s_wsl[wb][:],
                    in_=hap(wT_dram, 128 * oc,
                            [(D, 128), (128 * D, NCH), (1, 128)])),
                    inc=s_w[wb], k=16)
                b.wait("vector", s_w[wb], s_w[wb].n)
                b.emit("vector", lambda e, wb=wb: e.tensor_scalar(
                    s_wsl[wb][:], s_wsl[wb][:], 0.5, None, ALU.mult))
                wready = s_dve.n
                pair = oc % 3
                b.wait("tensor", s_dve, max(wready, z_done,
                                            pe_mark.get((tag, "evac", oc - 3), 0)))
            for dc in range(NCH):
                for h in range(2):
                    b.emit("tensor", lambda e, wb=wb, dc=dc, h=h, pair=pair: e.matmul(
                        p_all[0:128, 1024 * pair + 512 * h:1024 * pair + 512 * h + 512],
                        s_wsl[wb][:, 128 * dc:128 * dc + 128],
                        lnT[dc][:, 512 * h:512 * h + 512],
                        start=(dc == 0), stop=False), inc=s_pe)
            for h in range(2):
                b.emit("tensor", lambda e, oc=oc, h=h, pair=pair: e.matmul(
                    p_all[0:128, 1024 * pair + 512 * h:1024 * pair + 512 * h + 512],
                    s_Bf[:, 128 * oc:128 * oc + 128],
                    s_Z[:, 512 * h:512 * h + 512], start=False, stop=True), inc=s_pe)
            pe_mark[(tag, "base", oc)] = s_pe.n
            xb2 = oc % 3
            xq = "gpsimd" if sync_w else "sync"
            xss = s_gx if sync_w else s_xs
            sso = s_gso if sync_w else s_so
            if oc == 0 and sync_w:
                for ss in sso:
                    b.wait(xq, ss, ss.n)
            b.wait(xq, s_dve, max(apply_mark,
                                  pe_mark.get((tag, "xload", oc - 3), 0)))
            b.emit(xq, lambda e, oc=oc, t=s_x[xb2]: e.dma_start(
                out=t[:], in_=src_dram[128 * oc:128 * oc + 128, :]),
                inc=xss[xb2], k=16)
            b.wait("vector", xss[xb2], xss[xb2].n)
            b.wait("vector", s_pe, s_pe.n)
            ob = oc % 2
            b.wait("vector", sso[ob], sso[ob].n)
            b.emit("vector", lambda e, pair=pair, ti=s_x[xb2], to=s_xo[ob]:
                   e.tensor_tensor(to[:], ti[:],
                                   p_all[0:128, 1024 * pair:1024 * pair + 1024],
                                   ALU.add))
            pe_mark[(tag, "evac", oc)] = s_dve.n
            pe_mark[(tag, "xload", oc)] = s_dve.n
            b.wait(xq, s_dve, s_dve.n)
            b.emit(xq, lambda e, oc=oc, t=s_xo[ob]: e.dma_start(
                out=dst_dram[128 * oc:128 * oc + 128, :], in_=t[:]),
                inc=sso[ob], k=16)
            if post_oc is not None:
                post_oc(oc, s_xo[ob])

    open_phase()
    s_x = [pa(f"sx1{i}", [128, T], F32) for i in range(3)]
    s_xo = [pa(f"sxo1{i}", [128, T], F32) for i in range(2)]
    xb1 = [pa(f"xb{i}", [128, T], BF16) for i in range(NCH)]
    sq1 = [pa(f"sq{i}", [128, T], BF16) for i in range(2)]
    tp1 = [pa(f"tmp{i}", [128, T], F32) for i in range(2)]
    fs1 = (pa("slgT1", [8, T], F32), pa("swT81", [8, T], BF16),
           pa("swexp1", [128, T], BF16), pa("sZ1", [128, T], BF16))
    am = layer_norm("l1", xin, xb1, sq1, tp1)

    def ln2_chunk(oc, xo_tile):
        b.emit("vector", lambda e, oc=oc, t=xo_tile: e.tensor_copy(
            xb1[oc][:], t[:]))
        pe_mark[("l2cast", oc)] = s_dve.n

    ffn("f1", xin, x1d, wfT, am, fs1, post_oc=ln2_chunk)
    for c in range(NCH):
        sqb = c % 2
        b.wait("vector", s_pe, pe_mark.get(("l2sq", c - 2), 0))
        b.emit("vector", lambda e, c=c, sqb=sqb: e.tensor_tensor(
            sq1[sqb][:], xb1[c][:], xb1[c][:], ALU.mult))
        b.wait("tensor", s_dve, s_dve.n)
        if c == 0:
            b.wait("tensor", s_dve, pe_mark[("f1", "evac", NCH - 1)])
        for h in range(2):
            b.emit("tensor", lambda e, c=c, h=h: e.matmul(
                p_all[0:1, 512 * h:512 * h + 512], s_ones128[:],
                xb1[c][:, 512 * h:512 * h + 512],
                start=(c == 0), stop=(c == NCH - 1)), inc=s_pe)
        for h in range(2):
            b.emit("tensor", lambda e, h=h, sqb=sqb, c=c: e.matmul(
                p_all[0:1, 1024 + 512 * h:1536 + 512 * h], s_ones128[:],
                sq1[sqb][:, 512 * h:512 * h + 512],
                start=(c == 0), stop=(c == NCH - 1)), inc=s_pe)
        pe_mark[("l2sq", c)] = s_pe.n
    b.wait("vector", s_pe, s_pe.n)
    M = s_rows[0:1, 0:1024]
    E2 = s_rows[0:1, 1024:2048]
    T3 = s_rows[0:1, 2048:3072]
    V4 = s_rows[0:1, 3072:4096]
    b.emit("vector", lambda e: e.tensor_scalar(
        M, p_all[0:1, 0:1024], 1.0 / 2048.0, None, ALU.mult))
    b.emit("vector", lambda e: e.tensor_scalar(
        E2, p_all[0:1, 1024:2048], 1.0 / 2048.0, None, ALU.mult))
    b.emit("vector", lambda e: e.tensor_tensor(T3, M, M, ALU.mult))
    b.emit("vector", lambda e: e.tensor_tensor(V4, E2, T3, ALU.subtract))
    b.wait("scalar", s_dve, s_dve.n)
    b.emit("scalar", lambda e: e.activation(E2, V4, AF.Ln, bias=EPS, scale=1.0))
    b.emit("scalar", lambda e: e.activation(T3, E2, AF.Exp, scale=-0.5))
    b.wait("vector", s_act, s_act.n)
    b.emit("vector", lambda e: e.tensor_tensor(V4, M, T3, ALU.mult))
    b.wait("tensor", s_dve, s_dve.n)
    for h in range(2):
        b.emit("tensor", lambda e, h=h: e.matmul(
            p_all[0:128, 512 * h:512 * h + 512], s_onesF[0:1, :],
            s_rows[0:1, 2048 + 512 * h:2048 + 512 * h + 512],
            start=True, stop=True), inc=s_pe)
        b.emit("tensor", lambda e, h=h: e.matmul(
            p_all[0:128, 1024 + 512 * h:1536 + 512 * h], s_onesF[0:1, :],
            s_rows[0:1, 3072 + 512 * h:3072 + 512 * h + 512],
            start=True, stop=True), inc=s_pe)
    pe_exp2 = s_pe.n
    for c in range(NCH):
        tb2 = c % 2
        b.wait("vector", s_pe, pe_exp2)
        b.emit("vector", lambda e, c=c, tb2=tb2: e.tensor_tensor(
            tp1[tb2][:], xb1[c][:], p_all[0:128, 1024:2048], ALU.subtract))
        b.emit("vector", lambda e, c=c, tb2=tb2: e.tensor_tensor(
            lnT[c][:], tp1[tb2][:], p_all[0:128, 0:1024], ALU.mult))
    am2 = s_dve.n
    barrier()
    if dbg:
        b.emit("gpsimd", lambda e: e.dma_start(out=dbg_out["d_x1"], in_=x1d),
               inc=s_st, k=16)
    close_phase()

    open_phase()
    if dbg:
        b.wait("sync", s_dve, s_dve.n)
        for c in range(NCH):
            b.emit("sync", lambda e, c=c: e.dma_start(
                out=dbg_out["d_ln2"][128 * c:128 * c + 128, :], in_=lnT[c][:]),
                inc=s_st, k=16)
    s_wq = [pa(f"wq{i}", [128, D], BF16) for i in range(2)]
    s_wk = [pa(f"wk{i}", [128, D], BF16) for i in range(2)]
    s_wv = [pa(f"wv{i}", [128, D], BF16) for i in range(2)]
    s_qsl = pa("qsl", [128, T], BF16)
    s_ksl = pa("ksl", [128, T], BF16)
    s_vft = pa("vft", [128, T], BF16)
    s_vtok = [pa(f"vtok{i}", [128, 2080], BF16) for i in range(8)]

    for i in range(8):
        b.emit("vector", lambda e, i=i: e.memset(s_vtok[i][:], 1.0))
    vtok_ones = s_dve.n
    b.wait("gpsimd", s_dve, am2)
    for hp in range(HP):
        wb = hp % 2
        b.wait("gpsimd", s_pe, pe_mark.get(("qkv", hp - 2), 0))
        for wdram, sbufs in ((wqT, s_wq), (wkT, s_wk), (wvT, s_wv)):
            b.emit("gpsimd", lambda e, wdram=wdram, sbufs=sbufs, wb=wb, hp=hp: e.dma_start(
                out=sbufs[wb][:],
                in_=hap(wdram, 128 * hp, [(D, 128), (128 * D, NCH), (1, 128)])),
                inc=s_qkvw[wb], k=16)
        b.wait("tensor", s_qkvw[wb], s_qkvw[wb].n)
        b.wait("tensor", s_dve, max(am2, pe_mark.get(("qkvev", hp - 1), 0)))
        for sl, bk in ((s_wq, 0), (s_wk, 2), (s_wv, 4)):
            for dc in range(NCH):
                for h in range(2):
                    b.emit("tensor", lambda e, sl=sl, wb=wb, dc=dc, h=h, bk=bk: e.matmul(
                        p_all[0:128, 512 * (bk + h):512 * (bk + h) + 512],
                        sl[wb][:, 128 * dc:128 * dc + 128],
                        lnT[dc][:, 512 * h:512 * h + 512],
                        start=(dc == 0), stop=(dc == NCH - 1)), inc=s_pe)
        pe_mark[("qkv", hp)] = s_pe.n
        b.wait("vector", s_pe, s_pe.n)
        b.wait("vector", s_qk, s_qk.n)
        b.emit("vector", lambda e: e.tensor_copy(s_qsl[:], p_all[0:128, 0:1024]))
        b.emit("vector", lambda e: e.tensor_copy(s_ksl[:], p_all[0:128, 1024:2048]),
   )
        b.emit("vector", lambda e: e.tensor_copy(s_vft[:], p_all[0:128, 2048:3072]),
   )
        ev = s_dve.n
        pe_mark[("qkvev", hp)] = ev
        b.wait("sync", s_dve, ev)
        b.emit("sync", lambda e, hp=hp: e.dma_start(
            out=qtd[128 * hp:128 * hp + 128, :], in_=s_qsl[:]), inc=s_qk, k=16)
        b.emit("sync", lambda e, hp=hp: e.dma_start(
            out=kbs[hp // 4].ap()[128 * (hp % 4):128 * (hp % 4) + 128, :],
            in_=s_ksl[:]), inc=s_qk, k=16)
        b.wait("tensor", s_dve, max(ev, vtok_ones))
        for tc in range(8):
            b.wait("tensor", s_dve, pe_mark.get(("vtp", hp, tc - 3), 0))
            q4 = tc % 4
            b.emit("tensor", lambda e, tc=tc, q4=q4: e.transpose(
                p_all[0:128, 3072 + 64 * q4:3072 + 64 * q4 + 64].bitcast(BF16),
                s_vft[:, 128 * tc:128 * tc + 128], s_idb[:]), inc=s_pe)
            b.wait("vector", s_pe, s_pe.n)
            dst = s_vtok[tc][:, 130 * hp:130 * hp + 130].rearrange(
                "p (h c) -> p h c", h=2, c=65)[:, :, 0:64]
            src = p_all[0:128, 3072 + 64 * q4:3072 + 64 * q4 + 64].bitcast(
                BF16).rearrange("p (h c) -> p h c", h=2, c=64)
            b.emit("vector", lambda e, dst=dst, src=src: e.tensor_copy(dst, src),
       )
            pe_mark[("vtp", hp, tc)] = s_dve.n
        pe_mark[("qkve", hp)] = s_dve.n
        if hp % 4 == 3:
            st8 = hp // 4
            b.wait("sync", s_dve, s_dve.n)
            for tc in range(8):
                b.emit("sync", lambda e, tc=tc, st8=st8: e.dma_start(
                    out=vbs[st8].ap()[128 * tc:128 * tc + 128, :],
                    in_=s_vtok[tc][:, 520 * st8:520 * st8 + 520]), inc=s_st, k=16)
            b.wait("gpsimd", s_st, s_st.n)
            b.wait("gpsimd", s_qk, s_qk.n)
            b.emit("gpsimd", lambda e, st8=st8: e.collective_compute(
                "AllGather", ALU.bypass, replica_groups=GROUPS,
                ins=[kbs[st8].ap().opt()], outs=[kgs[st8].ap().opt()]), inc=s_cc)
            b.emit("gpsimd", lambda e, st8=st8: e.collective_compute(
                "AllGather", ALU.bypass, replica_groups=GROUPS,
                ins=[vbs[st8].ap().opt()], outs=[vgs[st8].ap().opt()]), inc=s_cc)
    cc_done = s_cc.n
    if dbg:
        b.emit("gpsimd", lambda e: e.dma_start(out=dbg_out["d_qt"], in_=qtd),
               inc=s_st, k=16)
    barrier()
    close_phase()

    open_phase()
    for oc in range(NCH):
        b.emit("gpsimd", lambda e, oc=oc: e.dma_start(
            out=wosh[128 * oc:128 * oc + 128, :],
            in_=hap(woT, 128 * oc, [(D, 128), (128 * D, NCH), (1, 128)])),
            inc=s_wsh, k=16)
        b.emit("gpsimd", lambda e, oc=oc: e.dma_start(
            out=wfsh[128 * oc:128 * oc + 128, :],
            in_=hap(wfT, 128 * oc, [(D, 128), (128 * D, NCH), (1, 128)])),
            inc=s_fsh, k=16)
    s_KT = [pa(f"aKT{i}", [128, 4096], BF16) for i in range(2)]
    s_QTt = [pa(f"aQT{i}", [128, T], BF16) for i in range(2)]
    s_vaug = [pa(f"aVA{i}", [128, 4160], BF16) for i in range(2)]
    s_pt = [pa(f"aPT{i}", [128, T], BF16) for i in range(4)]
    s_peb = [pa(f"aPB{i}", [128, T], BF16) for i in range(4)]
    s_ctxf = pa("aCF", [65, T], F32)
    s_ctxn = pa("aCN", [64, T], BF16)
    def load_hp(hp, buf):
        st8 = hp // 4
        b.wait("sync", s_cc, 2 * (st8 + 1))
        b.emit("sync", lambda e, hp=hp, buf=buf, st8=st8: e.dma_start(
            out=s_KT[buf][:],
            in_=hap(kgs[st8].ap(), (128 * (hp % 4)) * T,
                    [(T, 128), (512 * T, 4), (1, T)])), inc=s_att[buf], k=16)
        b.emit("sync", lambda e, hp=hp, buf=buf: e.dma_start(
            out=s_QTt[buf][:], in_=qtd[128 * hp:128 * hp + 128, :]), inc=s_att[buf], k=16)
        b.emit("sync", lambda e, hp=hp, buf=buf, st8=st8: e.dma_start(
            out=s_vaug[buf][:],
            in_=hap(vgs[st8].ap(), 130 * (hp % 4),
                    [(520, 128), (128 * 520, KC), (1, 130)])), inc=s_att[buf], k=16)

    load_hp(0, 0)
    for hp in range(HP):
        buf = hp % 2
        hp_ld = s_att[buf].n
        if hp + 1 < HP:
            b.wait("sync", s_pe, pe_mark.get(("att", hp - 1), 0))
            load_hp(hp + 1, (hp + 1) % 2)
        b.wait("tensor", s_att[buf], hp_ld)

        def mul_mark(kc, qh):
            if kc < 0:
                return pe_mark.get(("attmul", hp - 1, KC + kc, qh), 0)
            return pe_mark[("attmul", hp, kc, qh)]

        def exp_mark(kc, qh):
            return pe_mark.get(("attexp", hp, kc, qh),
                               pe_mark.get(("attexp", hp - 1, KC - 1, qh), 0))

        def emit_ctx(kc):
            for qh in range(2):
                b.wait("tensor", s_dve, mul_mark(kc, qh))
                if kc == 0:
                    b.wait("tensor", s_dve, pe_mark.get(("attev", hp - 1), 0))
                for hh in range(2):
                    b.emit("tensor", lambda e, kc=kc, qh=qh, hh=hh, buf=buf: e.matmul(
                        p_all[0:65, 512 * (4 + 2 * hh + qh):512 * (4 + 2 * hh + qh) + 512],
                        s_vaug[buf][:, 130 * kc + 65 * hh:130 * kc + 65 * hh + 65],
                        s_peb[2 * (kc % 2) + qh][:, 512 * hh:512 * hh + 512],
                        start=(kc == 0), stop=(kc == KC - 1)), inc=s_pe)
                pe_mark[("attctx", hp, kc, qh)] = s_pe.n

        for kc in range(KC):
            for qh in range(2):
                b.wait("tensor", s_act, exp_mark(kc - 1, qh))
                for hh in range(2):
                    b.emit("tensor", lambda e, kc=kc, qh=qh, hh=hh, buf=buf: e.matmul(
                        p_all[0:128, 1024 * qh + 512 * hh:1024 * qh + 512 * hh + 512],
                        s_KT[buf][64 * hh:64 * hh + 64, 128 * kc:128 * kc + 128],
                        s_QTt[buf][64 * hh:64 * hh + 64, 512 * qh:512 * qh + 512],
                        start=True, stop=True, tile_position=(64 * hh, 0)), inc=s_pe)
                pe_mark[("attsmm", hp, kc, qh)] = s_pe.n
            if kc >= 1:
                emit_ctx(kc - 1)
            for qh in range(2):
                ptb = 2 * (kc % 2) + qh
                b.wait("scalar", s_pe, pe_mark[("attsmm", hp, kc, qh)])
                b.wait("scalar", s_dve, mul_mark(kc - 2, qh))
                b.emit("scalar", lambda e, qh=qh, ptb=ptb: e.activation(
                    s_pt[ptb][:], p_all[0:128, 1024 * qh:1024 * qh + 1024], AF.Exp,
                    bias=-SHIFT, scale=0.125))
                pe_mark[("attexp", hp, kc, qh)] = s_act.n
            for qh in range(2):
                ptb = 2 * (kc % 2) + qh
                b.wait("vector", s_act, pe_mark[("attexp", hp, kc, qh)])
                b.wait("vector", s_pe,
                       pe_mark.get(("attctx", hp, kc - 2, qh),
                                   pe_mark.get(("att", hp - 1), 0)))
                rep = hap(ebt[kc].ap(), 512 * qh, [(T, 128), (0, 2), (1, 512)])
                pbb = 2 * (kc % 2) + qh
                b.emit("vector", lambda e, pbb=pbb, ptb=ptb, rep=rep: e.tensor_tensor(
                    s_peb[pbb][:], s_pt[ptb][:], rep, ALU.mult))
                pe_mark[("attmul", hp, kc, qh)] = s_dve.n
        emit_ctx(KC - 1)
        pe_mark[("att", hp)] = s_pe.n
        for hh in range(2):
            b.wait("vector", s_pe, pe_mark[("att", hp)])
            b.emit("vector", lambda e, hh=hh: e.tensor_copy(
                s_ctxf[:, 0:512],
                p_all[0:65, 512 * (4 + 2 * hh):512 * (4 + 2 * hh) + 512]))
            b.emit("vector", lambda e, hh=hh: e.tensor_copy(
                s_ctxf[:, 512:1024],
                p_all[0:65, 512 * (5 + 2 * hh):512 * (5 + 2 * hh) + 512]))
            ev_done = s_dve.n
            b.wait("scalar", s_dve, ev_done)
            b.emit("scalar", lambda e: e.activation(
                s_rows[0:1, 1024:2048], s_ctxf[64:65, :], AF.Ln))
            b.emit("scalar", lambda e: e.activation(
                s_rows[0:1, 0:1024], s_rows[0:1, 1024:2048], AF.Exp, scale=-1.0))
            b.wait("tensor", s_act, s_act.n)
            b.wait("tensor", s_dve, ev_done)
            for h2 in range(2):
                b.emit("tensor", lambda e, h2=h2, hh=hh: e.matmul(
                    p_all[0:64, 512 * (4 + 2 * hh) + 512 * h2:
                          512 * (4 + 2 * hh) + 512 * h2 + 512],
                    s_onesF[0:1, 0:64],
                    s_rows[0:1, 512 * h2:512 * h2 + 512], start=True, stop=True),
                    inc=s_pe)
            b.wait("vector", s_pe, s_pe.n)
            b.wait("vector", s_ctxst, s_ctxst.n)
            b.emit("vector", lambda e, hh=hh: e.tensor_tensor(
                s_ctxn[:], s_ctxf[0:64, :],
                p_all[0:64, 512 * (4 + 2 * hh):512 * (4 + 2 * hh) + 1024], ALU.mult))
            b.wait("sync", s_dve, s_dve.n)
            b.emit("sync", lambda e, hp=hp, hh=hh: e.dma_start(
                out=ctxd[128 * hp + 64 * hh:128 * hp + 64 * hh + 64, :], in_=s_ctxn[:]),
                inc=s_ctxst, k=16)
        pe_mark[("attev", hp)] = s_dve.n
    if dbg:
        b.wait("gpsimd", s_st, s_st.n)
        b.emit("gpsimd", lambda e: e.dma_start(out=dbg_out["d_ctx"], in_=ctxd),
               inc=s_st, k=16)
    barrier()
    close_phase()
    ebt_scope.close()

    open_phase()
    s_x = [pa(f"sx4{i}", [128, T], F32) for i in range(3)]
    s_xo = [pa(f"sxo4{i}", [128, T], F32) for i in range(2)]
    xb3 = [pa(f"xd{i}", [128, T], BF16) for i in range(NCH)]
    ctx_scope = contextlib.ExitStack()
    s_ctxall = [ctx_scope.enter_context(nc.sbuf_tensor(f"oc{i}", [128, T], BF16))
                for i in range(NCH)]
    for c in range(NCH):
        b.emit("gpsimd", lambda e, c=c: e.dma_start(
            out=s_ctxall[c][:], in_=ctxd[128 * c:128 * c + 128, :]), inc=s_gld, k=16)
    ctx_ld = s_gld.n
    for oc in range(NCH):
        wb = oc % 2
        if oc < 2:
            b.wait("sync", s_wsh, 256)
        b.wait("sync", s_pe, pe_mark.get(("opj", oc - 2), 0))
        b.emit("sync", lambda e, oc=oc, wb=wb: e.dma_start(
            out=s_wsl[wb][:],
            in_=wosh[128 * oc:128 * oc + 128, :]),
            inc=s_wh[wb], k=16)
        b.wait("tensor", s_wh[wb], s_wh[wb].n)
        b.wait("tensor", s_gld, ctx_ld)
        pair = oc % 3
        b.wait("tensor", s_dve, pe_mark.get(("opje", oc - 3), 0))
        for dc in range(NCH):
            for h in range(2):
                b.emit("tensor", lambda e, wb=wb, dc=dc, h=h, pair=pair: e.matmul(
                    p_all[0:128, 1024 * pair + 512 * h:1024 * pair + 512 * h + 512],
                    s_wsl[wb][:, 128 * dc:128 * dc + 128],
                    s_ctxall[dc][:, 512 * h:512 * h + 512],
                    start=(dc == 0), stop=(dc == NCH - 1)), inc=s_pe)
        pe_mark[("opj", oc)] = s_pe.n
        xb2 = oc % 3
        b.wait("gpsimd", s_dve, pe_mark.get(("opjx", oc - 3), 0))
        b.emit("gpsimd", lambda e, oc=oc, t=s_x[xb2]: e.dma_start(
            out=t[:], in_=x1d[128 * oc:128 * oc + 128, :]), inc=s_gx[xb2], k=16)
        b.wait("vector", s_gx[xb2], s_gx[xb2].n)
        b.wait("vector", s_pe, s_pe.n)
        ob = oc % 2
        b.wait("vector", s_gso[ob], s_gso[ob].n)
        b.emit("vector", lambda e, pair=pair, ti=s_x[xb2], to=s_xo[ob]:
               e.tensor_tensor(to[:], ti[:],
                               p_all[0:128, 1024 * pair:1024 * pair + 1024],
                               ALU.add))
        pe_mark[("opje", oc)] = s_dve.n
        pe_mark[("opjx", oc)] = s_dve.n
        b.wait("gpsimd", s_dve, s_dve.n)
        b.emit("gpsimd", lambda e, oc=oc, t=s_xo[ob]: e.dma_start(
            out=x2d[128 * oc:128 * oc + 128, :], in_=t[:]), inc=s_gso[ob], k=16)
        b.emit("vector", lambda e, oc=oc, t=s_xo[ob]: e.tensor_copy(
            xb3[oc][:], t[:]))
    if dbg:
        b.wait("gpsimd", s_st, s_st.n)
        b.emit("gpsimd", lambda e: e.dma_start(out=dbg_out["d_x2"], in_=x2d),
               inc=s_st, k=16)
    ctx_scope.close()
    sq3 = [pa(f"sq3{i}", [128, T], BF16) for i in range(2)]
    tp3 = [pa("tmp3", [128, T], F32)]
    fs3 = (pa("slgT3", [8, T], F32), pa("swT83", [8, T], BF16),
           pa("swexp3", [128, T], BF16), pa("sZ3", [128, T], BF16))
    opj_all = pe_mark[("opj", NCH - 1)]
    for c in range(NCH):
        sqb = c % 2
        b.wait("vector", s_pe, max(opj_all, pe_mark.get(("l3sq", c - 2), 0)))
        b.emit("vector", lambda e, c=c, sqb=sqb: e.tensor_tensor(
            sq3[sqb][:], xb3[c][:], xb3[c][:], ALU.mult))
        b.wait("tensor", s_dve, s_dve.n)
        if c == 0:
            b.wait("tensor", s_dve, pe_mark[("opje", NCH - 1)])
        for h in range(2):
            b.emit("tensor", lambda e, c=c, h=h: e.matmul(
                p_all[0:1, 512 * h:512 * h + 512], s_ones128[:],
                xb3[c][:, 512 * h:512 * h + 512],
                start=(c == 0), stop=(c == NCH - 1)), inc=s_pe)
        for h in range(2):
            b.emit("tensor", lambda e, h=h, sqb=sqb, c=c: e.matmul(
                p_all[0:1, 1024 + 512 * h:1536 + 512 * h], s_ones128[:],
                sq3[sqb][:, 512 * h:512 * h + 512],
                start=(c == 0), stop=(c == NCH - 1)), inc=s_pe)
        pe_mark[("l3sq", c)] = s_pe.n
    b.wait("vector", s_pe, s_pe.n)
    M = s_rows[0:1, 0:1024]
    E2 = s_rows[0:1, 1024:2048]
    T3 = s_rows[0:1, 2048:3072]
    V4 = s_rows[0:1, 3072:4096]
    b.emit("vector", lambda e: e.tensor_scalar(
        M, p_all[0:1, 0:1024], 1.0 / 2048.0, None, ALU.mult))
    b.emit("vector", lambda e: e.tensor_scalar(
        E2, p_all[0:1, 1024:2048], 1.0 / 2048.0, None, ALU.mult))
    b.emit("vector", lambda e: e.tensor_tensor(T3, M, M, ALU.mult))
    b.emit("vector", lambda e: e.tensor_tensor(V4, E2, T3, ALU.subtract))
    b.wait("scalar", s_dve, s_dve.n)
    b.emit("scalar", lambda e: e.activation(E2, V4, AF.Ln, bias=EPS, scale=1.0))
    b.emit("scalar", lambda e: e.activation(T3, E2, AF.Exp, scale=-0.5))
    b.wait("vector", s_act, s_act.n)
    b.emit("vector", lambda e: e.tensor_tensor(V4, M, T3, ALU.mult))
    b.wait("tensor", s_dve, s_dve.n)
    for h in range(2):
        b.emit("tensor", lambda e, h=h: e.matmul(
            p_all[0:128, 512 * h:512 * h + 512], s_onesF[0:1, :],
            s_rows[0:1, 2048 + 512 * h:2048 + 512 * h + 512],
            start=True, stop=True), inc=s_pe)
        b.emit("tensor", lambda e, h=h: e.matmul(
            p_all[0:128, 1024 + 512 * h:1536 + 512 * h], s_onesF[0:1, :],
            s_rows[0:1, 3072 + 512 * h:3072 + 512 * h + 512],
            start=True, stop=True), inc=s_pe)
    pe_exp3 = s_pe.n
    for c in range(NCH):
        b.wait("vector", s_pe, pe_exp3)
        b.emit("vector", lambda e, c=c: e.tensor_tensor(
            tp3[0][:], xb3[c][:], p_all[0:128, 1024:2048], ALU.subtract))
        b.emit("vector", lambda e, c=c: e.tensor_tensor(
            lnT[c][:], tp3[0][:], p_all[0:128, 0:1024], ALU.mult))
    am3 = s_dve.n
    ffn("f2", x2d, xout, wfT, am3, fs3, w_guard=opj_all, sync_w=True)
    close_phase()

    for ss in (s_st, s_so[0], s_so[1]):
        b.wait("sync", ss, ss.n)
    for ss in (s_gso[0], s_gso[1]):
        b.wait("gpsimd", ss, ss.n)

    with nc.Block() as block:
        @block.sync
        def _(e):
            for f in b.streams["sync"]:
                f(e)

        @block.tensor
        def _(e):
            for f in b.streams["tensor"]:
                f(e)

        @block.vector
        def _(e):
            for f in b.streams["vector"]:
                f(e)

        @block.scalar
        def _(e):
            for f in b.streams["scalar"]:
                f(e)

        @block.gpsimd
        def _(e):
            for f in b.streams["gpsimd"]:
                f(e)

    ctx0.close()
    return nc


def make_inputs(full, core):
    bidx, s = core // 4, core % 4
    qb = 1024 * s
    x = np.asarray(full["L0_attn_hidden_states"])
    xT = np.ascontiguousarray(x[bidx, qb:qb + 1024, :].T)
    rel = np.asarray(full["L0_attn_rel_pos"])
    idx = np.clip(4606 - qb - np.arange(HREV), 0, 1022)
    relw = np.ascontiguousarray(rel[idx])
    E8 = np.zeros((8, 128), dtype=np.float32)
    for e in range(8):
        E8[e, 16 * e:16 * e + 16] = 1.0
    return dict(
        xin=xT,
        wqT=np.ascontiguousarray(np.asarray(full["L0_attn_wq"]).T),
        wkT=np.ascontiguousarray(np.asarray(full["L0_attn_wk"]).T),
        wvT=np.ascontiguousarray(np.asarray(full["L0_attn_wv"]).T),
        woT=np.ascontiguousarray(np.asarray(full["L0_attn_wo"]).T),
        wfT=np.ascontiguousarray(np.asarray(full["L0_ffn_weight"]).T),
        atT=np.ascontiguousarray(np.asarray(full["L0_ffn_lora_A"]).reshape(128, D).T),
        bff=np.ascontiguousarray(
            np.asarray(full["L0_ffn_lora_B"]).transpose(0, 2, 1).reshape(128, D)),
        rtT=np.ascontiguousarray(np.asarray(full["L0_ffn_router_w"]).T),
        relw=relw,
        jmat=np.eye(128, dtype=np.float32)[::-1].copy(),
        idm=np.eye(128, dtype=np.float32),
        e8m=E8,
    )


def kernel(**inputs):
    nc = build_nc(dbg=False)
    in_maps = [make_inputs(inputs, c) for c in range(8)]
    res = run_bass_kernel_spmd(nc, in_maps, core_ids=list(range(8)))
    out = np.empty((2, 4096, 2048), dtype=np.float32)
    for c in range(8):
        bidx, s = c // 4, c % 4
        out[bidx, 1024 * s:1024 * s + 1024, :] = np.asarray(res.results[c]["xout"]).T
    return out


if __name__ == "__main__":
    import jax
    cpu = jax.devices("cpu")[0]
    with jax.default_device(cpu):
        import reference as R
        ins = R.setup_inputs(0)
        ins = {k: np.asarray(v) for k, v in ins.items()}
        exp = np.asarray(R.reference(**ins))
    np.save("_dev_exp.npy", exp)
    got = kernel(**ins)
    num = float(np.mean((got - exp) ** 2))
    den = float(np.mean(exp ** 2))
    print("res-var ratio:", num / den)


# version 27
# speedup vs baseline: 1.0377x; 1.0377x over previous
import contextlib
import numpy as np
import ml_dtypes
import concourse.bass as bass
import concourse.mybir as mybir
import bass_rust
from concourse.bass_utils import run_bass_kernel_spmd

F32 = mybir.dt.float32
BF16 = mybir.dt.bfloat16
AF = mybir.ActivationFunctionType
ALU = mybir.AluOpType
AX = mybir.AxisListType.X

D = 2048
T = 1024
NCH = 16
HP = 16
KC = 32
GROUPS = [[0, 1, 2, 3], [4, 5, 6, 7]]
SHIFT = 20.0
EPS = 1e-5
BIG = 1e30
HREV = 5632

ENGINES = ("sync", "tensor", "vector", "scalar", "gpsimd")


class Sem:
    def __init__(self, handle):
        self.h = handle
        self.n = 0

    def inc(self, k=1):
        self.n += k
        return self.n


class B:

    def __init__(self):
        self.streams = {e: [] for e in ENGINES}
        self.autosem = {}

    def emit(self, eng, fn, inc=None, k=1, raw=False):
        a = None if raw else self.autosem.get(eng)
        if a is not None:
            assert inc is None or inc is a, "vector/scalar ops must inc engine sem"
            self.wait(eng, a, a.n)
            inc = a
            k = 1
        if inc is not None:
            inc.inc(k)
            self.streams[eng].append(lambda e, f=fn, s=inc.h, kk=k: f(e).then_inc(s, kk))
        else:
            self.streams[eng].append(fn)

    def wait(self, eng, sem, val):
        if val <= 0:
            return
        self.streams[eng].append(lambda e, s=sem.h, v=val: e.wait_ge(s, v))


def hap(base_ap, offset, dims):
    c = base_ap.copy()
    c.ap = bass_rust.VecI64Pair(list(dims))
    c.offset = offset
    return c


def build_nc(dbg=False):
    nc = bass.Bass("TRN2", target_bir_lowering=False)

    xin = nc.dram_tensor("xin", [D, T], F32, kind="ExternalInput").ap()
    wqT = nc.dram_tensor("wqT", [D, D], F32, kind="ExternalInput").ap()
    wkT = nc.dram_tensor("wkT", [D, D], F32, kind="ExternalInput").ap()
    wvT = nc.dram_tensor("wvT", [D, D], F32, kind="ExternalInput").ap()
    woT = nc.dram_tensor("woT", [D, D], F32, kind="ExternalInput").ap()
    wfT = nc.dram_tensor("wfT", [D, D], F32, kind="ExternalInput").ap()
    atT = nc.dram_tensor("atT", [D, 128], F32, kind="ExternalInput").ap()
    bff = nc.dram_tensor("bff", [128, D], F32, kind="ExternalInput").ap()
    rtT = nc.dram_tensor("rtT", [D, 8], F32, kind="ExternalInput").ap()
    relw = nc.dram_tensor("relw", [HREV, 64], F32, kind="ExternalInput").ap()
    jmat = nc.dram_tensor("jmat", [128, 128], F32, kind="ExternalInput").ap()
    idm = nc.dram_tensor("idm", [128, 128], F32, kind="ExternalInput").ap()
    e8m = nc.dram_tensor("e8m", [8, 128], F32, kind="ExternalInput").ap()

    xout = nc.dram_tensor("xout", [D, T], F32, kind="ExternalOutput").ap()

    x1d = nc.dram_tensor("x1d", [D, T], F32).ap()
    x2d = nc.dram_tensor("x2d", [D, T], F32).ap()
    qtd = nc.dram_tensor("qtd", [D, T], BF16).ap()
    ctxd = nc.dram_tensor("ctxd", [D, T], BF16).ap()
    hrevd = nc.dram_tensor("hrevd", [1, HREV], BF16).ap()
    kbs = [nc.dram_tensor(f"kb{s}", [512, T], BF16) for s in range(4)]
    kgs = [nc.dram_tensor(f"kg{s}", [4 * 512, T], BF16) for s in range(4)]
    vbs = [nc.dram_tensor(f"vb{s}", [T, 520], BF16) for s in range(4)]
    vgs = [nc.dram_tensor(f"vg{s}", [4 * T, 520], BF16) for s in range(4)]

    dbg_out = {}
    if dbg:
        for nm, shp, dt in [("d_x1", [D, T], F32), ("d_ln2", [D, T], BF16),
                            ("d_qt", [D, T], BF16), ("d_ebt", [128, T], BF16),
                            ("d_ctx", [D, T], BF16), ("d_x2", [D, T], F32),
                            ("d_lg", [128, 8], F32), ("d_w", [128, 8], BF16),
                            ("d_z", [128, T], BF16), ("d_hrev", [1, HREV], BF16)]:
            dbg_out[nm] = nc.dram_tensor(nm, shp, dt, kind="ExternalOutput").ap()

    b = B()
    ctx0 = contextlib.ExitStack()
    E = ctx0.enter_context

    names = ["pool", "pe", "act", "dve", "st", "cc", "bar", "init", "misc",
             "rel0", "rel1", "g0", "g1", "xs0", "xs1", "xs2", "w0", "w1",
             "qkvw0", "qkvw1", "att0", "att1", "ldctx", "so0", "so1", "qk", "ctxst",
             "wh0", "wh1", "gx0", "gx1", "gx2", "gso0", "gso1", "gld"]
    sem = {n: Sem(E(nc.semaphore(n))) for n in names}
    s_pool, s_pe, s_act, s_dve, s_st, s_cc, s_bar, s_init = (
        sem[n] for n in ["pool", "pe", "act", "dve", "st", "cc", "bar", "init"])
    s_misc = sem["misc"]
    s_rel = [sem["rel0"], sem["rel1"]]
    s_g = [sem["g0"], sem["g1"]]
    s_xs = [sem["xs0"], sem["xs1"], sem["xs2"]]
    s_w = [sem["w0"], sem["w1"]]
    s_qkvw = [sem["qkvw0"], sem["qkvw1"]]
    s_att = [sem["att0"], sem["att1"]]
    s_ldctx = sem["ldctx"]
    s_so = [sem["so0"], sem["so1"]]
    s_qk = sem["qk"]
    s_ctxst = sem["ctxst"]
    s_wh = [sem["wh0"], sem["wh1"]]
    s_gx = [sem["gx0"], sem["gx1"], sem["gx2"]]
    s_gso = [sem["gso0"], sem["gso1"]]
    s_gld = sem["gld"]
    b.autosem["vector"] = s_dve
    b.autosem["scalar"] = s_act

    lnT = [E(nc.sbuf_tensor(f"lnT{i}", [128, T], BF16)) for i in range(NCH)]
    s_ones128 = E(nc.sbuf_tensor("ones128", [128, 1], BF16))
    s_onesF = E(nc.sbuf_tensor("onesF", [128, 128], F32))
    s_J = E(nc.sbuf_tensor("sJ", [128, 128], BF16))
    s_idb = E(nc.sbuf_tensor("sidb", [128, 128], BF16))
    s_idf = E(nc.sbuf_tensor("sidf", [128, 128], F32))
    s_E8 = E(nc.sbuf_tensor("sE8", [8, 128], BF16))
    s_AT = [E(nc.sbuf_tensor(f"sAT{i}", [128, 128], BF16)) for i in range(NCH)]
    s_RT = [E(nc.sbuf_tensor(f"sRT{i}", [128, 8], BF16)) for i in range(NCH)]
    s_Bf = E(nc.sbuf_tensor("sBf", [128, D], BF16))
    s_wsl = [E(nc.sbuf_tensor(f"wsl{i}", [128, D], BF16)) for i in range(2)]
    s_x = None
    s_xo = None
    s_wf32 = None
    s_cst = E(nc.sbuf_tensor("scst", [128, 8], F32))
    s_rows = E(nc.sbuf_tensor("srows", [1, 4096], F32))
    s_r8 = E(nc.sbuf_tensor("sr8", [128, 16], F32))
    s_lg = E(nc.sbuf_tensor("slg", [128, 8], F32))
    s_lgb = [E(nc.sbuf_tensor(f"slgb{i}", [128, 8], F32)) for i in range(3)]
    s_w8 = E(nc.sbuf_tensor("sw8", [128, 8], BF16))
    ebt_scope = contextlib.ExitStack()
    ebt = [ebt_scope.enter_context(nc.sbuf_tensor(f"ebt{i}", [128, T], BF16))
           for i in range(KC)]

    CONSTS = [-SHIFT, 1.0 / 2048.0, EPS, BIG, -1.0, 0.5]
    for j, v in enumerate(CONSTS):
        nc.const_aps.register(F32, float(v), s_cst[:, j:j + 1])

    p_all = E(nc.psum_tensor("pall", [128, 4096], F32))

    pe_mark = {}
    phase_stack = [None]

    def open_phase():
        phase_stack[0] = contextlib.ExitStack()

    def close_phase():
        phase_stack[0].close()
        phase_stack[0] = None

    def pa(name, shape, dtype):
        return phase_stack[0].enter_context(nc.sbuf_tensor(name, shape, dtype))

    def barrier():
        tgt = s_bar.n + 5
        s_bar.n = tgt
        for eng in ENGINES:
            b.emit(eng, lambda e: e.sem_inc(s_bar.h, 1), raw=True)
        for eng in ENGINES:
            b.wait(eng, s_bar, tgt)
        for eng in ENGINES:
            for s2 in (s_pe, s_act, s_dve):
                b.wait(eng, s2, s2.n)
        for eng in ("sync", "gpsimd"):
            for ss in (s_st, s_so[0], s_so[1], s_qk, s_ctxst):
                b.wait(eng, ss, ss.n)

    b.emit("vector", lambda e: e.memset(s_ones128[:], 1.0))
    b.emit("vector", lambda e: e.memset(s_onesF[:], 1.0))
    for j, v in enumerate(CONSTS):
        b.emit("vector", lambda e, j=j, v=v: e.memset(s_cst[:, j:j + 1], float(v)))
    b.emit("vector", lambda e: e.memset(s_cst[:, 6:8], 0.0))
    init_done = s_dve.n
    for eng in ENGINES:
        if eng != "vector":
            b.wait(eng, s_dve, init_done)
    b.emit("gpsimd", lambda e: e.dma_start(out=s_J[:], in_=jmat), inc=s_pool, k=16)
    b.emit("gpsimd", lambda e: e.dma_start(out=s_idb[:], in_=idm), inc=s_pool, k=16)
    b.emit("gpsimd", lambda e: e.dma_start(out=s_E8[:], in_=e8m), inc=s_pool, k=16)
    b.emit("gpsimd", lambda e: e.dma_start(out=s_Bf[:], in_=bff), inc=s_pool, k=16)
    for i in range(NCH):
        b.emit("gpsimd", lambda e, i=i: e.dma_start(
            out=s_AT[i][:], in_=atT[128 * i:128 * i + 128, :]), inc=s_pool, k=16)
        b.emit("gpsimd", lambda e, i=i: e.dma_start(
            out=s_RT[i][:], in_=rtT[128 * i:128 * i + 128, :]), inc=s_pool, k=16)
    pool_init = s_pool.n
    b.emit("sync", lambda e: e.dma_start(out=s_idf[:], in_=idm), inc=s_misc, k=16)
    idf_ld = s_misc.n

    open_phase()
    nch0 = HREV // 128
    s_relall = pa("p0rel", [128, nch0 * 64], F32)
    s_rs = pa("p0rs", [128, nch0], F32)
    s_hrow = pa("p0hrow", [44, 128], BF16)
    b.emit("sync", lambda e: e.dma_start(
        out=s_relall[:],
        in_=hap(relw, 0, [(64, 128), (128 * 64, nch0), (1, 64)])), inc=s_rel[0], k=16)
    b.wait("vector", s_rel[0], s_rel[0].n)
    for i in range(nch0):
        b.emit("vector", lambda e, i=i: e.reduce_sum(
            s_rs[:, i:i + 1], s_relall[:, 64 * i:64 * i + 64], axis=AX))
    b.wait("tensor", s_dve, s_dve.n)
    b.wait("tensor", s_misc, idf_ld)
    b.emit("tensor", lambda e: e.transpose(
        p_all[0:nch0, 3584:3712], s_rs[:], s_idf[:]), inc=s_pe)
    b.wait("scalar", s_pe, s_pe.n)
    b.emit("scalar", lambda e: e.activation(
        s_hrow[:], p_all[0:nch0, 3584:3712], AF.Exp, scale=1.0 / 64.0))
    b.wait("sync", s_act, s_act.n)
    b.emit("sync", lambda e: e.dma_start(
        out=hap(hrevd, 0, [(128, 44), (1, 128)]), in_=s_hrow[:]), inc=s_st, k=16)
    if dbg:
        b.emit("sync", lambda e: e.dma_start(
            out=hap(dbg_out["d_hrev"], 0, [(128, 44), (1, 128)]), in_=s_hrow[:]),
            inc=s_st, k=16)
    hrev_st = s_st.n
    s_G = [pa(f"p0G{i}", [128, T], BF16) for i in range(2)]
    for kc in range(KC):
        bb = kc % 2
        qeng = "sync" if bb == 0 else "scalar"
        off = 128 * (31 - kc)
        b.wait(qeng, s_st, hrev_st)
        b.wait(qeng, s_pe, pe_mark.get(("p0G", kc - 2), 0))
        b.emit(qeng, lambda e, off=off, bb=bb: e.dma_start(
            out=s_G[bb][:], in_=hap(hrevd, off, [(1, 128), (1, T)])),
            inc=s_g[bb], k=16, raw=True)
        b.wait("tensor", s_g[bb], s_g[bb].n)
        b.wait("tensor", s_pool, pool_init)
        b.wait("tensor", s_dve, pe_mark.get(("p0ebv", kc - 1), 0))
        for h in range(2):
            b.emit("tensor", lambda e, bb=bb, h=h: e.matmul(
                p_all[0:128, 512 * (5 + h):512 * (5 + h) + 512], s_J[:],
                s_G[bb][:, 512 * h:512 * h + 512], start=True, stop=True), inc=s_pe)
        pe_mark[("p0G", kc)] = s_pe.n
        b.wait("vector", s_pe, s_pe.n)
        b.emit("vector", lambda e, kc=kc: e.tensor_copy(
            ebt[kc][:], p_all[0:128, 2560:3584]))
        pe_mark[("p0ebv", kc)] = s_dve.n
    if dbg:
        b.wait("sync", s_dve, s_dve.n)
        b.emit("sync", lambda e: e.dma_start(out=dbg_out["d_ebt"], in_=ebt[4][:]),
               inc=s_st, k=16)
    barrier()
    close_phase()

    def layer_norm(tag, src_dram, xb_tiles, s_sq, s_tmp):
        for c in range(NCH):
            xbuf = c % 3
            b.wait("sync", s_dve, pe_mark.get((tag, "cast", c - 3), 0))
            b.emit("sync", lambda e, c=c, t=s_x[xbuf]: e.dma_start(
                out=t[:], in_=src_dram[128 * c:128 * c + 128, :]),
                inc=s_xs[xbuf], k=16)
            b.wait("vector", s_xs[xbuf], s_xs[xbuf].n)
            b.emit("vector", lambda e, c=c, t=s_x[xbuf]: e.tensor_copy(
                xb_tiles[c][:], t[:]))
            pe_mark[(tag, "cast", c)] = s_dve.n
            sqb = c % 2
            b.wait("vector", s_pe, pe_mark.get((tag, "sqmm", c - 2), 0))
            b.emit("vector", lambda e, c=c, sqb=sqb: e.tensor_tensor(
                s_sq[sqb][:], xb_tiles[c][:], xb_tiles[c][:], ALU.mult))
            b.wait("tensor", s_dve, s_dve.n)
            for h in range(2):
                b.emit("tensor", lambda e, c=c, h=h: e.matmul(
                    p_all[0:1, 512 * h:512 * h + 512], s_ones128[:],
                    xb_tiles[c][:, 512 * h:512 * h + 512],
                    start=(c == 0), stop=(c == NCH - 1)), inc=s_pe)
            for h in range(2):
                b.emit("tensor", lambda e, h=h, sqb=sqb, c=c: e.matmul(
                    p_all[0:1, 1024 + 512 * h:1536 + 512 * h], s_ones128[:],
                    s_sq[sqb][:, 512 * h:512 * h + 512],
                    start=(c == 0), stop=(c == NCH - 1)), inc=s_pe)
            pe_mark[(tag, "sqmm", c)] = s_pe.n
        b.wait("vector", s_pe, s_pe.n)

        M = s_rows[0:1, 0:1024]
        E2 = s_rows[0:1, 1024:2048]
        T3 = s_rows[0:1, 2048:3072]
        V4 = s_rows[0:1, 3072:4096]

        b.emit("vector", lambda e: e.tensor_scalar(
            M, p_all[0:1, 0:1024], 1.0 / 2048.0, None, ALU.mult))
        b.emit("vector", lambda e: e.tensor_scalar(
            E2, p_all[0:1, 1024:2048], 1.0 / 2048.0, None, ALU.mult))
        b.emit("vector", lambda e: e.tensor_tensor(T3, M, M, ALU.mult))
        b.emit("vector", lambda e: e.tensor_tensor(V4, E2, T3, ALU.subtract))
        b.wait("scalar", s_dve, s_dve.n)

        b.emit("scalar", lambda e: e.activation(E2, V4, AF.Ln, bias=EPS, scale=1.0))
        b.emit("scalar", lambda e: e.activation(T3, E2, AF.Exp, scale=-0.5))
        b.wait("vector", s_act, s_act.n)
        b.emit("vector", lambda e: e.tensor_tensor(V4, M, T3, ALU.mult))
        b.wait("tensor", s_dve, s_dve.n)
        for h in range(2):
            b.emit("tensor", lambda e, h=h: e.matmul(
                p_all[0:128, 512 * h:512 * h + 512], s_onesF[0:1, :],
                s_rows[0:1, 2048 + 512 * h:2048 + 512 * h + 512],
                start=True, stop=True), inc=s_pe)
            b.emit("tensor", lambda e, h=h: e.matmul(
                p_all[0:128, 1024 + 512 * h:1536 + 512 * h], s_onesF[0:1, :],
                s_rows[0:1, 3072 + 512 * h:3072 + 512 * h + 512],
                start=True, stop=True), inc=s_pe)
        pe_exp = s_pe.n
        for c in range(NCH):
            tb = c % 2
            b.wait("vector", s_pe, pe_exp)
            b.emit("vector", lambda e, c=c, tb=tb: e.tensor_tensor(
                s_tmp[tb][:], xb_tiles[c][:], p_all[0:128, 1024:2048], ALU.subtract),
    )
            b.emit("vector", lambda e, c=c, tb=tb: e.tensor_tensor(
                lnT[c][:], s_tmp[tb][:], p_all[0:128, 0:1024], ALU.mult))
        return s_dve.n

    def ffn(tag, src_dram, dst_dram, wT_dram, apply_mark, fs, post_oc=None,
            w_guard=0, wf_guard=0, sync_w=False):
        s_lgT, s_wT8, s_wexp, s_Z = fs
        b.wait("tensor", s_dve, apply_mark)
        for c in range(NCH):
            b.emit("tensor", lambda e, c=c: e.matmul(
                p_all[0:128, 0:512], s_AT[c][:], lnT[c][:, 0:512],
                start=(c == 0), stop=(c == NCH - 1)), inc=s_pe)
            b.emit("tensor", lambda e, c=c: e.matmul(
                p_all[0:128, 512:1024], s_AT[c][:], lnT[c][:, 512:1024],
                start=(c == 0), stop=(c == NCH - 1)), inc=s_pe)
            b.emit("tensor", lambda e, c=c: e.matmul(
                p_all[0:8, 1024:1536], s_RT[c][:], lnT[c][:, 0:512],
                start=(c == 0), stop=(c == NCH - 1)), inc=s_pe)
            b.emit("tensor", lambda e, c=c: e.matmul(
                p_all[0:8, 1536:2048], s_RT[c][:], lnT[c][:, 512:1024],
                start=(c == 0), stop=(c == NCH - 1)), inc=s_pe)
        pe_lora = s_pe.n
        b.wait("vector", s_pe, pe_lora)
        b.emit("vector", lambda e: e.tensor_copy(s_lgT[:], p_all[0:8, 1024:2048]),
   )
        lgT_ev = s_dve.n
        for c in range(8):
            b.wait("tensor", s_dve, max(lgT_ev, pe_mark.get((tag, "lgdone", c - 4), 0)))
            b.emit("tensor", lambda e, c=c: e.transpose(
                p_all[0:128, 2048 + 8 * (c % 4):2048 + 8 * (c % 4) + 8],
                s_lgT[0:8, 128 * c:128 * c + 128], s_idf[0:8, 0:8]), inc=s_pe)
            b.wait("vector", s_pe, s_pe.n)
            col = 2048 + 8 * (c % 4)

            b.emit("vector", lambda e, col=col: e.tensor_copy(
                s_lg[:], p_all[0:128, col:col + 8]))
            b.emit("vector", lambda e: e.reduce_max(s_r8[:, 0:1], s_lg[:], axis=AX))
            b.emit("vector", lambda e: e.tensor_scalar(
                s_lgb[0][:], s_lg[:], s_r8[:, 0:1], None, ALU.is_ge))
            b.emit("vector", lambda e: e.tensor_scalar(
                s_lgb[0][:], s_lgb[0][:], BIG, None, ALU.mult))
            b.emit("vector", lambda e: e.tensor_tensor(
                s_lgb[1][:], s_lg[:], s_lgb[0][:], ALU.subtract))
            b.emit("vector", lambda e: e.reduce_max(s_r8[:, 1:2], s_lgb[1][:], axis=AX))
            b.emit("vector", lambda e: e.tensor_scalar(
                s_r8[:, 2:3], s_r8[:, 0:1], -1.0, None, ALU.mult))
            pe_mark[(tag, "lgdone", c)] = s_dve.n
            b.wait("scalar", s_dve, s_dve.n)
            b.emit("scalar", lambda e: e.activation(
                s_lgb[2][:], s_lg[:], AF.Exp, bias=s_r8[:, 2:3], scale=1.0))
            b.wait("vector", s_act, s_act.n)
            b.wait("vector", s_pe, pe_mark.get((tag, "wtp", c - 1), 0))

            b.emit("vector", lambda e: e.tensor_scalar(
                s_lgb[0][:], s_lg[:], s_r8[:, 1:2], None, ALU.is_ge))
            b.emit("vector", lambda e: e.tensor_tensor(
                s_lgb[1][:], s_lgb[2][:], s_lgb[0][:], ALU.mult))
            b.emit("vector", lambda e: e.reduce_sum(s_r8[:, 3:4], s_lgb[1][:], axis=AX))
            b.emit("vector", lambda e: e.reciprocal(s_r8[:, 4:5], s_r8[:, 3:4]))
            b.emit("vector", lambda e: e.tensor_scalar(
                s_w8[:], s_lgb[1][:], s_r8[:, 4:5], None, ALU.mult))
            if dbg and c == 0 and tag == "f1":
                b.wait("sync", s_dve, s_dve.n)
                b.emit("sync", lambda e: e.dma_start(out=dbg_out["d_lg"], in_=s_lg[:]),
                       inc=s_st, k=16)
                b.emit("gpsimd", lambda e: e.dma_start(out=dbg_out["d_w"], in_=s_w8[:]),
                       inc=s_st, k=16)
            b.wait("tensor", s_dve, s_dve.n)
            b.emit("tensor", lambda e: e.transpose(
                p_all[0:8, 2560:2624].bitcast(BF16), s_w8[:], s_idb[:]), inc=s_pe)
            pe_mark[(tag, "wtp", c)] = s_pe.n
            b.wait("vector", s_pe, s_pe.n)
            b.emit("vector", lambda e, c=c: e.tensor_copy(
                s_wT8[0:8, 128 * c:128 * c + 128], p_all[0:8, 2560:2624].bitcast(BF16)),
    )
        b.wait("tensor", s_dve, s_dve.n)
        for h in range(2):
            b.emit("tensor", lambda e, h=h: e.matmul(
                p_all[0:128, 3072 + 512 * h:3072 + 512 * h + 512], s_E8[:],
                s_wT8[0:8, 512 * h:512 * h + 512], start=True, stop=True), inc=s_pe)
        b.wait("vector", s_pe, s_pe.n)
        b.emit("vector", lambda e: e.tensor_copy(s_wexp[:], p_all[0:128, 3072:4096]),
   )
        b.emit("vector", lambda e: e.tensor_tensor(
            s_Z[:], p_all[0:128, 0:1024], s_wexp[:], ALU.mult))
        z_done = s_dve.n
        if dbg and tag == "f1":
            b.wait("sync", s_dve, z_done)
            b.emit("sync", lambda e: e.dma_start(out=dbg_out["d_z"], in_=s_Z[:]),
                   inc=s_st, k=16)
        for oc in range(NCH):
            wb = oc % 2
            if sync_w:
                b.wait("sync", s_act, max(wf_guard,
                                          pe_mark.get((tag, "wcast", oc - 2), 0)))
                b.emit("sync", lambda e, oc=oc, wf=s_wf32[wb]: e.dma_start(
                    out=wf[:],
                    in_=hap(wT_dram, 128 * oc,
                            [(D, 128), (128 * D, NCH), (1, 128)])),
                    inc=s_wh[wb], k=16)
                b.wait("scalar", s_wh[wb], s_wh[wb].n)
                b.wait("scalar", s_pe, max(w_guard,
                                           pe_mark.get((tag, "base", oc - 2), 0)))
                b.emit("scalar", lambda e, wb=wb, wf=s_wf32[wb]: e.activation(
                    s_wsl[wb][:], wf[:], AF.Copy, scale=0.5))
                pe_mark[(tag, "wcast", oc)] = s_act.n
                pair = oc % 3
                b.wait("tensor", s_act, pe_mark[(tag, "wcast", oc)])
                b.wait("tensor", s_dve, max(z_done,
                                            pe_mark.get((tag, "evac", oc - 3), 0)))
            else:
                b.wait("gpsimd", s_pe, max(w_guard,
                                            pe_mark.get((tag, "base", oc - 2), 0)))
                b.emit("gpsimd", lambda e, oc=oc, wb=wb: e.dma_start(
                    out=s_wsl[wb][:],
                    in_=hap(wT_dram, 128 * oc,
                            [(D, 128), (128 * D, NCH), (1, 128)])),
                    inc=s_w[wb], k=16)
                b.wait("vector", s_w[wb], s_w[wb].n)
                b.emit("vector", lambda e, wb=wb: e.tensor_scalar(
                    s_wsl[wb][:], s_wsl[wb][:], 0.5, None, ALU.mult))
                wready = s_dve.n
                pair = oc % 3
                b.wait("tensor", s_dve, max(wready, z_done,
                                            pe_mark.get((tag, "evac", oc - 3), 0)))
            for dc in range(NCH):
                for h in range(2):
                    b.emit("tensor", lambda e, wb=wb, dc=dc, h=h, pair=pair: e.matmul(
                        p_all[0:128, 1024 * pair + 512 * h:1024 * pair + 512 * h + 512],
                        s_wsl[wb][:, 128 * dc:128 * dc + 128],
                        lnT[dc][:, 512 * h:512 * h + 512],
                        start=(dc == 0), stop=False), inc=s_pe)
            for h in range(2):
                b.emit("tensor", lambda e, oc=oc, h=h, pair=pair: e.matmul(
                    p_all[0:128, 1024 * pair + 512 * h:1024 * pair + 512 * h + 512],
                    s_Bf[:, 128 * oc:128 * oc + 128],
                    s_Z[:, 512 * h:512 * h + 512], start=False, stop=True), inc=s_pe)
            pe_mark[(tag, "base", oc)] = s_pe.n
            xb2 = oc % 3
            xq = "gpsimd" if sync_w else "sync"
            xss = s_gx if sync_w else s_xs
            sso = s_gso if sync_w else s_so
            if oc == 0 and sync_w:
                for ss in sso:
                    b.wait(xq, ss, ss.n)
            b.wait(xq, s_dve, max(apply_mark,
                                  pe_mark.get((tag, "xload", oc - 3), 0)))
            b.emit(xq, lambda e, oc=oc, t=s_x[xb2]: e.dma_start(
                out=t[:], in_=src_dram[128 * oc:128 * oc + 128, :]),
                inc=xss[xb2], k=16)
            b.wait("vector", xss[xb2], xss[xb2].n)
            b.wait("vector", s_pe, s_pe.n)
            ob = oc % 2
            b.wait("vector", sso[ob], sso[ob].n)
            b.emit("vector", lambda e, pair=pair, ti=s_x[xb2], to=s_xo[ob]:
                   e.tensor_tensor(to[:], ti[:],
                                   p_all[0:128, 1024 * pair:1024 * pair + 1024],
                                   ALU.add))
            pe_mark[(tag, "evac", oc)] = s_dve.n
            pe_mark[(tag, "xload", oc)] = s_dve.n
            b.wait(xq, s_dve, s_dve.n)
            b.emit(xq, lambda e, oc=oc, t=s_xo[ob]: e.dma_start(
                out=dst_dram[128 * oc:128 * oc + 128, :], in_=t[:]),
                inc=sso[ob], k=16)
            if post_oc is not None:
                post_oc(oc, s_xo[ob])

    open_phase()
    s_x = [pa(f"sx1{i}", [128, T], F32) for i in range(3)]
    s_xo = [pa(f"sxo1{i}", [128, T], F32) for i in range(2)]
    xb1 = [pa(f"xb{i}", [128, T], BF16) for i in range(NCH)]
    sq1 = [pa(f"sq{i}", [128, T], BF16) for i in range(2)]
    tp1 = [pa(f"tmp{i}", [128, T], F32) for i in range(2)]
    fs1 = (pa("slgT1", [8, T], F32), pa("swT81", [8, T], BF16),
           pa("swexp1", [128, T], BF16), pa("sZ1", [128, T], BF16))
    am = layer_norm("l1", xin, xb1, sq1, tp1)

    def ln2_chunk(oc, xo_tile):
        b.emit("vector", lambda e, oc=oc, t=xo_tile: e.tensor_copy(
            xb1[oc][:], t[:]))
        pe_mark[("l2cast", oc)] = s_dve.n

    ffn("f1", xin, x1d, wfT, am, fs1, post_oc=ln2_chunk)
    for c in range(NCH):
        sqb = c % 2
        b.wait("vector", s_pe, pe_mark.get(("l2sq", c - 2), 0))
        b.emit("vector", lambda e, c=c, sqb=sqb: e.tensor_tensor(
            sq1[sqb][:], xb1[c][:], xb1[c][:], ALU.mult))
        b.wait("tensor", s_dve, s_dve.n)
        if c == 0:
            b.wait("tensor", s_dve, pe_mark[("f1", "evac", NCH - 1)])
        for h in range(2):
            b.emit("tensor", lambda e, c=c, h=h: e.matmul(
                p_all[0:1, 512 * h:512 * h + 512], s_ones128[:],
                xb1[c][:, 512 * h:512 * h + 512],
                start=(c == 0), stop=(c == NCH - 1)), inc=s_pe)
        for h in range(2):
            b.emit("tensor", lambda e, h=h, sqb=sqb, c=c: e.matmul(
                p_all[0:1, 1024 + 512 * h:1536 + 512 * h], s_ones128[:],
                sq1[sqb][:, 512 * h:512 * h + 512],
                start=(c == 0), stop=(c == NCH - 1)), inc=s_pe)
        pe_mark[("l2sq", c)] = s_pe.n
    b.wait("vector", s_pe, s_pe.n)
    M = s_rows[0:1, 0:1024]
    E2 = s_rows[0:1, 1024:2048]
    T3 = s_rows[0:1, 2048:3072]
    V4 = s_rows[0:1, 3072:4096]
    b.emit("vector", lambda e: e.tensor_scalar(
        M, p_all[0:1, 0:1024], 1.0 / 2048.0, None, ALU.mult))
    b.emit("vector", lambda e: e.tensor_scalar(
        E2, p_all[0:1, 1024:2048], 1.0 / 2048.0, None, ALU.mult))
    b.emit("vector", lambda e: e.tensor_tensor(T3, M, M, ALU.mult))
    b.emit("vector", lambda e: e.tensor_tensor(V4, E2, T3, ALU.subtract))
    b.wait("scalar", s_dve, s_dve.n)
    b.emit("scalar", lambda e: e.activation(E2, V4, AF.Ln, bias=EPS, scale=1.0))
    b.emit("scalar", lambda e: e.activation(T3, E2, AF.Exp, scale=-0.5))
    b.wait("vector", s_act, s_act.n)
    b.emit("vector", lambda e: e.tensor_tensor(V4, M, T3, ALU.mult))
    b.wait("tensor", s_dve, s_dve.n)
    for h in range(2):
        b.emit("tensor", lambda e, h=h: e.matmul(
            p_all[0:128, 512 * h:512 * h + 512], s_onesF[0:1, :],
            s_rows[0:1, 2048 + 512 * h:2048 + 512 * h + 512],
            start=True, stop=True), inc=s_pe)
        b.emit("tensor", lambda e, h=h: e.matmul(
            p_all[0:128, 1024 + 512 * h:1536 + 512 * h], s_onesF[0:1, :],
            s_rows[0:1, 3072 + 512 * h:3072 + 512 * h + 512],
            start=True, stop=True), inc=s_pe)
    pe_exp2 = s_pe.n
    for c in range(NCH):
        tb2 = c % 2
        b.wait("vector", s_pe, pe_exp2)
        b.emit("vector", lambda e, c=c, tb2=tb2: e.tensor_tensor(
            tp1[tb2][:], xb1[c][:], p_all[0:128, 1024:2048], ALU.subtract))
        b.emit("vector", lambda e, c=c, tb2=tb2: e.tensor_tensor(
            lnT[c][:], tp1[tb2][:], p_all[0:128, 0:1024], ALU.mult))
    am2 = s_dve.n
    barrier()
    if dbg:
        b.emit("gpsimd", lambda e: e.dma_start(out=dbg_out["d_x1"], in_=x1d),
               inc=s_st, k=16)
    close_phase()

    open_phase()
    if dbg:
        b.wait("sync", s_dve, s_dve.n)
        for c in range(NCH):
            b.emit("sync", lambda e, c=c: e.dma_start(
                out=dbg_out["d_ln2"][128 * c:128 * c + 128, :], in_=lnT[c][:]),
                inc=s_st, k=16)
    s_wq = [pa(f"wq{i}", [128, D], BF16) for i in range(2)]
    s_wk = [pa(f"wk{i}", [128, D], BF16) for i in range(2)]
    s_wv = [pa(f"wv{i}", [128, D], BF16) for i in range(2)]
    s_qsl = pa("qsl", [128, T], BF16)
    s_ksl = pa("ksl", [128, T], BF16)
    s_vft = pa("vft", [128, T], BF16)
    s_vtok = [pa(f"vtok{i}", [128, 2080], BF16) for i in range(8)]

    for i in range(8):
        b.emit("vector", lambda e, i=i: e.memset(s_vtok[i][:], 1.0))
    vtok_ones = s_dve.n
    b.wait("gpsimd", s_dve, am2)
    for hp in range(HP):
        wb = hp % 2
        b.wait("gpsimd", s_pe, pe_mark.get(("qkv", hp - 2), 0))
        for wdram, sbufs in ((wqT, s_wq), (wkT, s_wk), (wvT, s_wv)):
            b.emit("gpsimd", lambda e, wdram=wdram, sbufs=sbufs, wb=wb, hp=hp: e.dma_start(
                out=sbufs[wb][:],
                in_=hap(wdram, 128 * hp, [(D, 128), (128 * D, NCH), (1, 128)])),
                inc=s_qkvw[wb], k=16)
        b.wait("tensor", s_qkvw[wb], s_qkvw[wb].n)
        b.wait("tensor", s_dve, max(am2, pe_mark.get(("qkvev", hp - 1), 0)))
        for sl, bk in ((s_wq, 0), (s_wk, 2), (s_wv, 4)):
            for dc in range(NCH):
                for h in range(2):
                    b.emit("tensor", lambda e, sl=sl, wb=wb, dc=dc, h=h, bk=bk: e.matmul(
                        p_all[0:128, 512 * (bk + h):512 * (bk + h) + 512],
                        sl[wb][:, 128 * dc:128 * dc + 128],
                        lnT[dc][:, 512 * h:512 * h + 512],
                        start=(dc == 0), stop=(dc == NCH - 1)), inc=s_pe)
        pe_mark[("qkv", hp)] = s_pe.n
        b.wait("vector", s_pe, s_pe.n)
        b.wait("vector", s_qk, s_qk.n)
        b.emit("vector", lambda e: e.tensor_copy(s_qsl[:], p_all[0:128, 0:1024]))
        b.emit("vector", lambda e: e.tensor_copy(s_ksl[:], p_all[0:128, 1024:2048]),
   )
        b.emit("vector", lambda e: e.tensor_copy(s_vft[:], p_all[0:128, 2048:3072]),
   )
        ev = s_dve.n
        pe_mark[("qkvev", hp)] = ev
        b.wait("sync", s_dve, ev)
        b.emit("sync", lambda e, hp=hp: e.dma_start(
            out=qtd[128 * hp:128 * hp + 128, :], in_=s_qsl[:]), inc=s_qk, k=16)
        b.emit("sync", lambda e, hp=hp: e.dma_start(
            out=kbs[hp // 4].ap()[128 * (hp % 4):128 * (hp % 4) + 128, :],
            in_=s_ksl[:]), inc=s_qk, k=16)
        b.wait("tensor", s_dve, max(ev, vtok_ones))
        for tc in range(8):
            b.wait("tensor", s_dve, pe_mark.get(("vtp", hp, tc - 3), 0))
            q4 = tc % 4
            b.emit("tensor", lambda e, tc=tc, q4=q4: e.transpose(
                p_all[0:128, 3072 + 64 * q4:3072 + 64 * q4 + 64].bitcast(BF16),
                s_vft[:, 128 * tc:128 * tc + 128], s_idb[:]), inc=s_pe)
            b.wait("vector", s_pe, s_pe.n)
            dst = s_vtok[tc][:, 130 * hp:130 * hp + 130].rearrange(
                "p (h c) -> p h c", h=2, c=65)[:, :, 0:64]
            src = p_all[0:128, 3072 + 64 * q4:3072 + 64 * q4 + 64].bitcast(
                BF16).rearrange("p (h c) -> p h c", h=2, c=64)
            b.emit("vector", lambda e, dst=dst, src=src: e.tensor_copy(dst, src),
       )
            pe_mark[("vtp", hp, tc)] = s_dve.n
        pe_mark[("qkve", hp)] = s_dve.n
        if hp % 4 == 3:
            st8 = hp // 4
            b.wait("sync", s_dve, s_dve.n)
            for tc in range(8):
                b.emit("sync", lambda e, tc=tc, st8=st8: e.dma_start(
                    out=vbs[st8].ap()[128 * tc:128 * tc + 128, :],
                    in_=s_vtok[tc][:, 520 * st8:520 * st8 + 520]), inc=s_st, k=16)
            b.wait("gpsimd", s_st, s_st.n)
            b.wait("gpsimd", s_qk, s_qk.n)
            b.emit("gpsimd", lambda e, st8=st8: e.collective_compute(
                "AllGather", ALU.bypass, replica_groups=GROUPS,
                ins=[kbs[st8].ap().opt()], outs=[kgs[st8].ap().opt()]), inc=s_cc)
            b.emit("gpsimd", lambda e, st8=st8: e.collective_compute(
                "AllGather", ALU.bypass, replica_groups=GROUPS,
                ins=[vbs[st8].ap().opt()], outs=[vgs[st8].ap().opt()]), inc=s_cc)
    cc_done = s_cc.n
    if dbg:
        b.emit("gpsimd", lambda e: e.dma_start(out=dbg_out["d_qt"], in_=qtd),
               inc=s_st, k=16)
    barrier()
    close_phase()

    open_phase()
    s_KT = [pa(f"aKT{i}", [128, 4096], BF16) for i in range(2)]
    s_QTt = [pa(f"aQT{i}", [128, T], BF16) for i in range(2)]
    s_vaug = [pa(f"aVA{i}", [128, 4160], BF16) for i in range(2)]
    s_pt = [pa(f"aPT{i}", [128, T], BF16) for i in range(4)]
    s_peb = [pa(f"aPB{i}", [128, T], BF16) for i in range(4)]
    s_ctxf = [pa(f"aCF{i}", [65, T], F32) for i in range(2)]
    s_ctxn = pa("aCN", [64, T], BF16)
    def load_hp(hp, buf):
        st8 = hp // 4
        b.wait("sync", s_cc, 2 * (st8 + 1))
        b.emit("sync", lambda e, hp=hp, buf=buf, st8=st8: e.dma_start(
            out=s_KT[buf][:],
            in_=hap(kgs[st8].ap(), (128 * (hp % 4)) * T,
                    [(T, 128), (512 * T, 4), (1, T)])), inc=s_att[buf], k=16)
        b.emit("sync", lambda e, hp=hp, buf=buf: e.dma_start(
            out=s_QTt[buf][:], in_=qtd[128 * hp:128 * hp + 128, :]), inc=s_att[buf], k=16)
        b.emit("sync", lambda e, hp=hp, buf=buf, st8=st8: e.dma_start(
            out=s_vaug[buf][:],
            in_=hap(vgs[st8].ap(), 130 * (hp % 4),
                    [(520, 128), (128 * 520, KC), (1, 130)])), inc=s_att[buf], k=16)

    load_hp(0, 0)
    pending = None
    for hp in range(HP):
        buf = hp % 2
        hp_ld = s_att[buf].n
        if hp + 1 < HP:
            b.wait("sync", s_pe, pe_mark.get(("att", hp - 1), 0))
            load_hp(hp + 1, (hp + 1) % 2)
        b.wait("tensor", s_att[buf], hp_ld)

        def mul_mark(kc, qh):
            if kc < 0:
                return pe_mark.get(("attmul", hp - 1, KC + kc, qh), 0)
            return pe_mark[("attmul", hp, kc, qh)]

        def exp_mark(kc, qh):
            return pe_mark.get(("attexp", hp, kc, qh),
                               pe_mark.get(("attexp", hp - 1, KC - 1, qh), 0))

        def emit_ctx(kc):
            for qh in range(2):
                b.wait("tensor", s_dve, mul_mark(kc, qh))
                if kc == 0:
                    b.wait("tensor", s_dve, pe_mark.get(("attev", hp - 1), 0))
                for hh in range(2):
                    b.emit("tensor", lambda e, kc=kc, qh=qh, hh=hh, buf=buf: e.matmul(
                        p_all[0:65, 512 * (4 + 2 * hh + qh):512 * (4 + 2 * hh + qh) + 512],
                        s_vaug[buf][:, 130 * kc + 65 * hh:130 * kc + 65 * hh + 65],
                        s_peb[2 * (kc % 2) + qh][:, 512 * hh:512 * hh + 512],
                        start=(kc == 0), stop=(kc == KC - 1)), inc=s_pe)
                pe_mark[("attctx", hp, kc, qh)] = s_pe.n

        for kc in range(KC):
            for qh in range(2):
                b.wait("tensor", s_act, exp_mark(kc - 1, qh))
                for hh in range(2):
                    b.emit("tensor", lambda e, kc=kc, qh=qh, hh=hh, buf=buf: e.matmul(
                        p_all[0:128, 1024 * qh + 512 * hh:1024 * qh + 512 * hh + 512],
                        s_KT[buf][64 * hh:64 * hh + 64, 128 * kc:128 * kc + 128],
                        s_QTt[buf][64 * hh:64 * hh + 64, 512 * qh:512 * qh + 512],
                        start=True, stop=True, tile_position=(64 * hh, 0)), inc=s_pe)
                pe_mark[("attsmm", hp, kc, qh)] = s_pe.n
            if kc == 0 and pending is not None:
                pending()
                pending = None
            if kc >= 1:
                emit_ctx(kc - 1)
            for qh in range(2):
                ptb = 2 * (kc % 2) + qh
                b.wait("scalar", s_pe, pe_mark[("attsmm", hp, kc, qh)])
                b.wait("scalar", s_dve, mul_mark(kc - 2, qh))
                b.emit("scalar", lambda e, qh=qh, ptb=ptb: e.activation(
                    s_pt[ptb][:], p_all[0:128, 1024 * qh:1024 * qh + 1024], AF.Exp,
                    bias=-SHIFT, scale=0.125))
                pe_mark[("attexp", hp, kc, qh)] = s_act.n
            for qh in range(2):
                ptb = 2 * (kc % 2) + qh
                b.wait("vector", s_act, pe_mark[("attexp", hp, kc, qh)])
                b.wait("vector", s_pe,
                       pe_mark.get(("attctx", hp, kc - 2, qh),
                                   pe_mark.get(("att", hp - 1), 0)))
                rep = hap(ebt[kc].ap(), 512 * qh, [(T, 128), (0, 2), (1, 512)])
                pbb = 2 * (kc % 2) + qh
                b.emit("vector", lambda e, pbb=pbb, ptb=ptb, rep=rep: e.tensor_tensor(
                    s_peb[pbb][:], s_pt[ptb][:], rep, ALU.mult))
                pe_mark[("attmul", hp, kc, qh)] = s_dve.n
        emit_ctx(KC - 1)
        pe_mark[("att", hp)] = s_pe.n
        for hh in range(2):
            b.wait("vector", s_pe, pe_mark[("att", hp)])
            b.emit("vector", lambda e, hh=hh: e.tensor_copy(
                s_ctxf[hh][:, 0:512],
                p_all[0:65, 512 * (4 + 2 * hh):512 * (4 + 2 * hh) + 512]))
            b.emit("vector", lambda e, hh=hh: e.tensor_copy(
                s_ctxf[hh][:, 512:1024],
                p_all[0:65, 512 * (5 + 2 * hh):512 * (5 + 2 * hh) + 512]))
        pe_mark[("attevac", hp)] = s_dve.n
        b.wait("scalar", s_dve, s_dve.n)
        b.wait("scalar", s_pe, pe_mark.get(("attxp", hp - 1), 0))
        b.emit("scalar", lambda e: e.activation(
            s_rows[0:1, 1024:2048], s_ctxf[0][64:65, :], AF.Ln))
        b.emit("scalar", lambda e: e.activation(
            s_rows[0:1, 0:1024], s_rows[0:1, 1024:2048], AF.Exp, scale=-1.0))
        b.emit("scalar", lambda e: e.activation(
            s_rows[0:1, 2048:3072], s_ctxf[1][64:65, :], AF.Ln))
        b.emit("scalar", lambda e: e.activation(
            s_rows[0:1, 3072:4096], s_rows[0:1, 2048:3072], AF.Exp, scale=-1.0))
        pe_mark[("attrc", hp)] = s_act.n

        def flush_B(hp=hp):
            b.wait("tensor", s_act, pe_mark[("attrc", hp)])
            b.wait("tensor", s_dve, pe_mark[("attevac", hp)])
            for hh in range(2):
                base = 0 if hh == 0 else 3072
                for h2 in range(2):
                    b.emit("tensor", lambda e, h2=h2, hh=hh, base=base: e.matmul(
                        p_all[0:64, 512 * (4 + 2 * hh) + 512 * h2:
                              512 * (4 + 2 * hh) + 512 * h2 + 512],
                        s_onesF[0:1, 0:64],
                        s_rows[0:1, base + 512 * h2:base + 512 * h2 + 512],
                        start=True, stop=True), inc=s_pe)
            pe_mark[("attxp", hp)] = exp_mk = s_pe.n
            for hh in range(2):
                b.wait("vector", s_pe, exp_mk)
                b.wait("vector", s_ctxst, s_ctxst.n)
                b.emit("vector", lambda e, hh=hh: e.tensor_tensor(
                    s_ctxn[:], s_ctxf[hh][0:64, :],
                    p_all[0:64, 512 * (4 + 2 * hh):512 * (4 + 2 * hh) + 1024],
                    ALU.mult))
                b.wait("sync", s_dve, s_dve.n)
                b.emit("sync", lambda e, hp=hp, hh=hh: e.dma_start(
                    out=ctxd[128 * hp + 64 * hh:128 * hp + 64 * hh + 64, :],
                    in_=s_ctxn[:]), inc=s_ctxst, k=16)
            pe_mark[("attev", hp)] = s_dve.n
        pending = flush_B
    pending()
    pending = None
    if dbg:
        b.wait("gpsimd", s_st, s_st.n)
        b.emit("gpsimd", lambda e: e.dma_start(out=dbg_out["d_ctx"], in_=ctxd),
               inc=s_st, k=16)
    barrier()
    close_phase()
    ebt_scope.close()

    open_phase()
    s_x = [pa(f"sx4{i}", [128, T], F32) for i in range(3)]
    s_xo = [pa(f"sxo4{i}", [128, T], F32) for i in range(2)]
    s_wf32 = [pa(f"wf4{i}", [128, D], F32) for i in range(2)]
    xb3 = [pa(f"xd{i}", [128, T], BF16) for i in range(NCH)]
    ctx_scope = contextlib.ExitStack()
    s_ctxall = [ctx_scope.enter_context(nc.sbuf_tensor(f"oc{i}", [128, T], BF16))
                for i in range(NCH)]
    for c in range(NCH):
        b.emit("gpsimd", lambda e, c=c: e.dma_start(
            out=s_ctxall[c][:], in_=ctxd[128 * c:128 * c + 128, :]), inc=s_gld, k=16)
    ctx_ld = s_gld.n
    for oc in range(NCH):
        wb = oc % 2
        b.wait("sync", s_act, pe_mark.get(("opjcast", oc - 2), 0))
        b.emit("sync", lambda e, oc=oc, wf=s_wf32[wb]: e.dma_start(
            out=wf[:],
            in_=hap(woT, 128 * oc, [(D, 128), (128 * D, NCH), (1, 128)])),
            inc=s_wh[wb], k=16)
        b.wait("scalar", s_wh[wb], s_wh[wb].n)
        b.wait("scalar", s_pe, pe_mark.get(("opj", oc - 2), 0))
        b.emit("scalar", lambda e, wb=wb, wf=s_wf32[wb]: e.activation(
            s_wsl[wb][:], wf[:], AF.Copy, scale=1.0))
        pe_mark[("opjcast", oc)] = s_act.n
        b.wait("tensor", s_act, pe_mark[("opjcast", oc)])
        b.wait("tensor", s_gld, ctx_ld)
        pair = oc % 3
        b.wait("tensor", s_dve, pe_mark.get(("opje", oc - 3), 0))
        for dc in range(NCH):
            for h in range(2):
                b.emit("tensor", lambda e, wb=wb, dc=dc, h=h, pair=pair: e.matmul(
                    p_all[0:128, 1024 * pair + 512 * h:1024 * pair + 512 * h + 512],
                    s_wsl[wb][:, 128 * dc:128 * dc + 128],
                    s_ctxall[dc][:, 512 * h:512 * h + 512],
                    start=(dc == 0), stop=(dc == NCH - 1)), inc=s_pe)
        pe_mark[("opj", oc)] = s_pe.n
        xb2 = oc % 3
        b.wait("gpsimd", s_dve, pe_mark.get(("opjx", oc - 3), 0))
        b.emit("gpsimd", lambda e, oc=oc, t=s_x[xb2]: e.dma_start(
            out=t[:], in_=x1d[128 * oc:128 * oc + 128, :]), inc=s_gx[xb2], k=16)
        b.wait("vector", s_gx[xb2], s_gx[xb2].n)
        b.wait("vector", s_pe, s_pe.n)
        ob = oc % 2
        b.wait("vector", s_gso[ob], s_gso[ob].n)
        b.emit("vector", lambda e, pair=pair, ti=s_x[xb2], to=s_xo[ob]:
               e.tensor_tensor(to[:], ti[:],
                               p_all[0:128, 1024 * pair:1024 * pair + 1024],
                               ALU.add))
        pe_mark[("opje", oc)] = s_dve.n
        pe_mark[("opjx", oc)] = s_dve.n
        b.wait("gpsimd", s_dve, s_dve.n)
        b.emit("gpsimd", lambda e, oc=oc, t=s_xo[ob]: e.dma_start(
            out=x2d[128 * oc:128 * oc + 128, :], in_=t[:]), inc=s_gso[ob], k=16)
        b.emit("vector", lambda e, oc=oc, t=s_xo[ob]: e.tensor_copy(
            xb3[oc][:], t[:]))
    if dbg:
        b.wait("gpsimd", s_st, s_st.n)
        b.emit("gpsimd", lambda e: e.dma_start(out=dbg_out["d_x2"], in_=x2d),
               inc=s_st, k=16)
    ctx_scope.close()
    sq3 = [pa(f"sq3{i}", [128, T], BF16) for i in range(2)]
    tp3 = [pa("tmp3", [128, T], F32)]
    fs3 = (pa("slgT3", [8, T], F32), pa("swT83", [8, T], BF16),
           pa("swexp3", [128, T], BF16), pa("sZ3", [128, T], BF16))
    opj_all = pe_mark[("opj", NCH - 1)]
    for c in range(NCH):
        sqb = c % 2
        b.wait("vector", s_pe, max(opj_all, pe_mark.get(("l3sq", c - 2), 0)))
        b.emit("vector", lambda e, c=c, sqb=sqb: e.tensor_tensor(
            sq3[sqb][:], xb3[c][:], xb3[c][:], ALU.mult))
        b.wait("tensor", s_dve, s_dve.n)
        if c == 0:
            b.wait("tensor", s_dve, pe_mark[("opje", NCH - 1)])
        for h in range(2):
            b.emit("tensor", lambda e, c=c, h=h: e.matmul(
                p_all[0:1, 512 * h:512 * h + 512], s_ones128[:],
                xb3[c][:, 512 * h:512 * h + 512],
                start=(c == 0), stop=(c == NCH - 1)), inc=s_pe)
        for h in range(2):
            b.emit("tensor", lambda e, h=h, sqb=sqb, c=c: e.matmul(
                p_all[0:1, 1024 + 512 * h:1536 + 512 * h], s_ones128[:],
                sq3[sqb][:, 512 * h:512 * h + 512],
                start=(c == 0), stop=(c == NCH - 1)), inc=s_pe)
        pe_mark[("l3sq", c)] = s_pe.n
    b.wait("vector", s_pe, s_pe.n)
    M = s_rows[0:1, 0:1024]
    E2 = s_rows[0:1, 1024:2048]
    T3 = s_rows[0:1, 2048:3072]
    V4 = s_rows[0:1, 3072:4096]
    b.emit("vector", lambda e: e.tensor_scalar(
        M, p_all[0:1, 0:1024], 1.0 / 2048.0, None, ALU.mult))
    b.emit("vector", lambda e: e.tensor_scalar(
        E2, p_all[0:1, 1024:2048], 1.0 / 2048.0, None, ALU.mult))
    b.emit("vector", lambda e: e.tensor_tensor(T3, M, M, ALU.mult))
    b.emit("vector", lambda e: e.tensor_tensor(V4, E2, T3, ALU.subtract))
    b.wait("scalar", s_dve, s_dve.n)
    b.emit("scalar", lambda e: e.activation(E2, V4, AF.Ln, bias=EPS, scale=1.0))
    b.emit("scalar", lambda e: e.activation(T3, E2, AF.Exp, scale=-0.5))
    b.wait("vector", s_act, s_act.n)
    b.emit("vector", lambda e: e.tensor_tensor(V4, M, T3, ALU.mult))
    b.wait("tensor", s_dve, s_dve.n)
    for h in range(2):
        b.emit("tensor", lambda e, h=h: e.matmul(
            p_all[0:128, 512 * h:512 * h + 512], s_onesF[0:1, :],
            s_rows[0:1, 2048 + 512 * h:2048 + 512 * h + 512],
            start=True, stop=True), inc=s_pe)
        b.emit("tensor", lambda e, h=h: e.matmul(
            p_all[0:128, 1024 + 512 * h:1536 + 512 * h], s_onesF[0:1, :],
            s_rows[0:1, 3072 + 512 * h:3072 + 512 * h + 512],
            start=True, stop=True), inc=s_pe)
    pe_exp3 = s_pe.n
    for c in range(NCH):
        b.wait("vector", s_pe, pe_exp3)
        b.emit("vector", lambda e, c=c: e.tensor_tensor(
            tp3[0][:], xb3[c][:], p_all[0:128, 1024:2048], ALU.subtract))
        b.emit("vector", lambda e, c=c: e.tensor_tensor(
            lnT[c][:], tp3[0][:], p_all[0:128, 0:1024], ALU.mult))
    am3 = s_dve.n
    ffn("f2", x2d, xout, wfT, am3, fs3, w_guard=opj_all,
        wf_guard=pe_mark[("opjcast", NCH - 1)], sync_w=True)
    close_phase()

    for ss in (s_st, s_so[0], s_so[1]):
        b.wait("sync", ss, ss.n)
    for ss in (s_gso[0], s_gso[1]):
        b.wait("gpsimd", ss, ss.n)

    with nc.Block() as block:
        @block.sync
        def _(e):
            for f in b.streams["sync"]:
                f(e)

        @block.tensor
        def _(e):
            for f in b.streams["tensor"]:
                f(e)

        @block.vector
        def _(e):
            for f in b.streams["vector"]:
                f(e)

        @block.scalar
        def _(e):
            for f in b.streams["scalar"]:
                f(e)

        @block.gpsimd
        def _(e):
            for f in b.streams["gpsimd"]:
                f(e)

    ctx0.close()
    return nc


def make_inputs(full, core):
    bidx, s = core // 4, core % 4
    qb = 1024 * s
    x = np.asarray(full["L0_attn_hidden_states"])
    xT = np.ascontiguousarray(x[bidx, qb:qb + 1024, :].T)
    rel = np.asarray(full["L0_attn_rel_pos"])
    idx = np.clip(4606 - qb - np.arange(HREV), 0, 1022)
    relw = np.ascontiguousarray(rel[idx])
    E8 = np.zeros((8, 128), dtype=np.float32)
    for e in range(8):
        E8[e, 16 * e:16 * e + 16] = 1.0
    return dict(
        xin=xT,
        wqT=np.ascontiguousarray(np.asarray(full["L0_attn_wq"]).T),
        wkT=np.ascontiguousarray(np.asarray(full["L0_attn_wk"]).T),
        wvT=np.ascontiguousarray(np.asarray(full["L0_attn_wv"]).T),
        woT=np.ascontiguousarray(np.asarray(full["L0_attn_wo"]).T),
        wfT=np.ascontiguousarray(np.asarray(full["L0_ffn_weight"]).T),
        atT=np.ascontiguousarray(np.asarray(full["L0_ffn_lora_A"]).reshape(128, D).T),
        bff=np.ascontiguousarray(
            np.asarray(full["L0_ffn_lora_B"]).transpose(0, 2, 1).reshape(128, D)),
        rtT=np.ascontiguousarray(np.asarray(full["L0_ffn_router_w"]).T),
        relw=relw,
        jmat=np.eye(128, dtype=np.float32)[::-1].copy(),
        idm=np.eye(128, dtype=np.float32),
        e8m=E8,
    )


def kernel(**inputs):
    nc = build_nc(dbg=False)
    in_maps = [make_inputs(inputs, c) for c in range(8)]
    res = run_bass_kernel_spmd(nc, in_maps, core_ids=list(range(8)))
    out = np.empty((2, 4096, 2048), dtype=np.float32)
    for c in range(8):
        bidx, s = c // 4, c % 4
        out[bidx, 1024 * s:1024 * s + 1024, :] = np.asarray(res.results[c]["xout"]).T
    return out


if __name__ == "__main__":
    import jax
    cpu = jax.devices("cpu")[0]
    with jax.default_device(cpu):
        import reference as R
        ins = R.setup_inputs(0)
        ins = {k: np.asarray(v) for k, v in ins.items()}
        exp = np.asarray(R.reference(**ins))
    np.save("_dev_exp.npy", exp)
    got = kernel(**ins)
    num = float(np.mean((got - exp) ** 2))
    den = float(np.mean(exp ** 2))
    print("res-var ratio:", num / den)


# version 28
# speedup vs baseline: 1.0473x; 1.0093x over previous
import contextlib
import numpy as np
import ml_dtypes
import concourse.bass as bass
import concourse.mybir as mybir
import bass_rust
from concourse.bass_utils import run_bass_kernel_spmd

F32 = mybir.dt.float32
BF16 = mybir.dt.bfloat16
AF = mybir.ActivationFunctionType
ALU = mybir.AluOpType
AX = mybir.AxisListType.X

D = 2048
T = 1024
NCH = 16
HP = 16
KC = 32
GROUPS = [[0, 1, 2, 3], [4, 5, 6, 7]]
SHIFT = 20.0
EPS = 1e-5
BIG = 1e30
HREV = 5632

ENGINES = ("sync", "tensor", "vector", "scalar", "gpsimd")


class Sem:
    def __init__(self, handle):
        self.h = handle
        self.n = 0

    def inc(self, k=1):
        self.n += k
        return self.n


class B:

    def __init__(self):
        self.streams = {e: [] for e in ENGINES}
        self.autosem = {}

    def emit(self, eng, fn, inc=None, k=1, raw=False):
        a = None if raw else self.autosem.get(eng)
        if a is not None:
            assert inc is None or inc is a, "vector/scalar ops must inc engine sem"
            self.wait(eng, a, a.n)
            inc = a
            k = 1
        if inc is not None:
            inc.inc(k)
            self.streams[eng].append(lambda e, f=fn, s=inc.h, kk=k: f(e).then_inc(s, kk))
        else:
            self.streams[eng].append(fn)

    def wait(self, eng, sem, val):
        if val <= 0:
            return
        self.streams[eng].append(lambda e, s=sem.h, v=val: e.wait_ge(s, v))


def hap(base_ap, offset, dims):
    c = base_ap.copy()
    c.ap = bass_rust.VecI64Pair(list(dims))
    c.offset = offset
    return c


def build_nc(dbg=False):
    nc = bass.Bass("TRN2", target_bir_lowering=False)

    xin = nc.dram_tensor("xin", [D, T], F32, kind="ExternalInput").ap()
    wqT = nc.dram_tensor("wqT", [D, D], F32, kind="ExternalInput").ap()
    wkT = nc.dram_tensor("wkT", [D, D], F32, kind="ExternalInput").ap()
    wvT = nc.dram_tensor("wvT", [D, D], F32, kind="ExternalInput").ap()
    woT = nc.dram_tensor("woT", [D, D], F32, kind="ExternalInput").ap()
    wfT = nc.dram_tensor("wfT", [D, D], F32, kind="ExternalInput").ap()
    atT = nc.dram_tensor("atT", [D, 128], F32, kind="ExternalInput").ap()
    bff = nc.dram_tensor("bff", [128, D], F32, kind="ExternalInput").ap()
    rtT = nc.dram_tensor("rtT", [D, 8], F32, kind="ExternalInput").ap()
    relw = nc.dram_tensor("relw", [HREV, 64], F32, kind="ExternalInput").ap()
    jmat = nc.dram_tensor("jmat", [128, 128], F32, kind="ExternalInput").ap()
    idm = nc.dram_tensor("idm", [128, 128], F32, kind="ExternalInput").ap()
    e8m = nc.dram_tensor("e8m", [8, 128], F32, kind="ExternalInput").ap()

    xout = nc.dram_tensor("xout", [D, T], F32, kind="ExternalOutput").ap()

    x1d = nc.dram_tensor("x1d", [D, T], F32).ap()
    x2d = nc.dram_tensor("x2d", [D, T], F32).ap()
    qtd = nc.dram_tensor("qtd", [D, T], BF16).ap()
    ctxd = nc.dram_tensor("ctxd", [D, T], BF16).ap()
    hrevd = nc.dram_tensor("hrevd", [1, HREV], BF16).ap()
    kbs = [nc.dram_tensor(f"kb{s}", [512, T], BF16) for s in range(4)]
    kgs = [nc.dram_tensor(f"kg{s}", [4 * 512, T], BF16) for s in range(4)]
    vbs = [nc.dram_tensor(f"vb{s}", [T, 520], BF16) for s in range(4)]
    vgs = [nc.dram_tensor(f"vg{s}", [4 * T, 520], BF16) for s in range(4)]

    dbg_out = {}
    if dbg:
        for nm, shp, dt in [("d_x1", [D, T], F32), ("d_ln2", [D, T], BF16),
                            ("d_qt", [D, T], BF16), ("d_ebt", [128, T], BF16),
                            ("d_ctx", [D, T], BF16), ("d_x2", [D, T], F32),
                            ("d_lg", [128, 8], F32), ("d_w", [128, 8], BF16),
                            ("d_z", [128, T], BF16), ("d_hrev", [1, HREV], BF16)]:
            dbg_out[nm] = nc.dram_tensor(nm, shp, dt, kind="ExternalOutput").ap()

    b = B()
    ctx0 = contextlib.ExitStack()
    E = ctx0.enter_context

    names = ["pool", "pe", "act", "dve", "st", "cc", "bar", "init", "misc",
             "rel0", "rel1", "g0", "g1", "xs0", "xs1", "xs2", "w0", "w1",
             "qkvw0", "qkvw1", "att0", "att1", "ldctx", "so0", "so1", "qk", "ctxst",
             "wh0", "wh1", "gx0", "gx1", "gx2", "gso0", "gso1", "gld"]
    sem = {n: Sem(E(nc.semaphore(n))) for n in names}
    s_pool, s_pe, s_act, s_dve, s_st, s_cc, s_bar, s_init = (
        sem[n] for n in ["pool", "pe", "act", "dve", "st", "cc", "bar", "init"])
    s_misc = sem["misc"]
    s_rel = [sem["rel0"], sem["rel1"]]
    s_g = [sem["g0"], sem["g1"]]
    s_xs = [sem["xs0"], sem["xs1"], sem["xs2"]]
    s_w = [sem["w0"], sem["w1"]]
    s_qkvw = [sem["qkvw0"], sem["qkvw1"]]
    s_att = [sem["att0"], sem["att1"]]
    s_ldctx = sem["ldctx"]
    s_so = [sem["so0"], sem["so1"]]
    s_qk = sem["qk"]
    s_ctxst = sem["ctxst"]
    s_wh = [sem["wh0"], sem["wh1"]]
    s_gx = [sem["gx0"], sem["gx1"], sem["gx2"]]
    s_gso = [sem["gso0"], sem["gso1"]]
    s_gld = sem["gld"]
    b.autosem["vector"] = s_dve
    b.autosem["scalar"] = s_act

    lnT = [E(nc.sbuf_tensor(f"lnT{i}", [128, T], BF16)) for i in range(NCH)]
    s_ones128 = E(nc.sbuf_tensor("ones128", [128, 1], BF16))
    s_onesF = E(nc.sbuf_tensor("onesF", [128, 128], F32))
    s_J = E(nc.sbuf_tensor("sJ", [128, 128], BF16))
    s_idb = E(nc.sbuf_tensor("sidb", [128, 128], BF16))
    s_idf = E(nc.sbuf_tensor("sidf", [128, 128], F32))
    s_E8 = E(nc.sbuf_tensor("sE8", [8, 128], BF16))
    s_AT = [E(nc.sbuf_tensor(f"sAT{i}", [128, 128], BF16)) for i in range(NCH)]
    s_RT = [E(nc.sbuf_tensor(f"sRT{i}", [128, 8], BF16)) for i in range(NCH)]
    s_Bf = E(nc.sbuf_tensor("sBf", [128, D], BF16))
    s_wsl = [E(nc.sbuf_tensor(f"wsl{i}", [128, D], BF16)) for i in range(2)]
    s_x = None
    s_xo = None
    s_wf32 = None
    s_cst = E(nc.sbuf_tensor("scst", [128, 8], F32))
    s_rows = E(nc.sbuf_tensor("srows", [1, 4096], F32))
    s_r8 = E(nc.sbuf_tensor("sr8", [128, 16], F32))
    s_lg = E(nc.sbuf_tensor("slg", [128, 8], F32))
    s_lgb = [E(nc.sbuf_tensor(f"slgb{i}", [128, 8], F32)) for i in range(3)]
    s_w8 = E(nc.sbuf_tensor("sw8", [128, 8], BF16))
    ebt_scope = contextlib.ExitStack()
    ebt = [ebt_scope.enter_context(nc.sbuf_tensor(f"ebt{i}", [128, T], BF16))
           for i in range(KC)]

    CONSTS = [-SHIFT, 1.0 / 2048.0, EPS, BIG, -1.0, 0.5]
    for j, v in enumerate(CONSTS):
        nc.const_aps.register(F32, float(v), s_cst[:, j:j + 1])

    p_all = E(nc.psum_tensor("pall", [128, 4096], F32))

    pe_mark = {}
    phase_stack = [None]

    def open_phase():
        phase_stack[0] = contextlib.ExitStack()

    def close_phase():
        phase_stack[0].close()
        phase_stack[0] = None

    def pa(name, shape, dtype):
        return phase_stack[0].enter_context(nc.sbuf_tensor(name, shape, dtype))

    def barrier():
        tgt = s_bar.n + 5
        s_bar.n = tgt
        for eng in ENGINES:
            b.emit(eng, lambda e: e.sem_inc(s_bar.h, 1), raw=True)
        for eng in ENGINES:
            b.wait(eng, s_bar, tgt)
        for eng in ENGINES:
            for s2 in (s_pe, s_act, s_dve):
                b.wait(eng, s2, s2.n)
        for eng in ("sync", "gpsimd"):
            for ss in (s_st, s_so[0], s_so[1], s_qk, s_ctxst):
                b.wait(eng, ss, ss.n)

    b.emit("vector", lambda e: e.memset(s_ones128[:], 1.0))
    b.emit("vector", lambda e: e.memset(s_onesF[:], 1.0))
    for j, v in enumerate(CONSTS):
        b.emit("vector", lambda e, j=j, v=v: e.memset(s_cst[:, j:j + 1], float(v)))
    b.emit("vector", lambda e: e.memset(s_cst[:, 6:8], 0.0))
    init_done = s_dve.n
    for eng in ENGINES:
        if eng != "vector":
            b.wait(eng, s_dve, init_done)
    b.emit("gpsimd", lambda e: e.dma_start(out=s_J[:], in_=jmat), inc=s_pool, k=16)
    b.emit("gpsimd", lambda e: e.dma_start(out=s_idb[:], in_=idm), inc=s_pool, k=16)
    b.emit("gpsimd", lambda e: e.dma_start(out=s_E8[:], in_=e8m), inc=s_pool, k=16)
    b.emit("gpsimd", lambda e: e.dma_start(out=s_Bf[:], in_=bff), inc=s_pool, k=16)
    for i in range(NCH):
        b.emit("gpsimd", lambda e, i=i: e.dma_start(
            out=s_AT[i][:], in_=atT[128 * i:128 * i + 128, :]), inc=s_pool, k=16)
        b.emit("gpsimd", lambda e, i=i: e.dma_start(
            out=s_RT[i][:], in_=rtT[128 * i:128 * i + 128, :]), inc=s_pool, k=16)
    pool_init = s_pool.n
    b.emit("sync", lambda e: e.dma_start(out=s_idf[:], in_=idm), inc=s_misc, k=16)
    idf_ld = s_misc.n

    open_phase()
    nch0 = HREV // 128
    s_relall = pa("p0rel", [128, nch0 * 64], F32)
    s_rs = pa("p0rs", [128, nch0], F32)
    s_hrow = pa("p0hrow", [44, 128], BF16)
    b.emit("sync", lambda e: e.dma_start(
        out=s_relall[:],
        in_=hap(relw, 0, [(64, 128), (128 * 64, nch0), (1, 64)])), inc=s_rel[0], k=16)
    b.wait("vector", s_rel[0], s_rel[0].n)
    for i in range(nch0):
        b.emit("vector", lambda e, i=i: e.reduce_sum(
            s_rs[:, i:i + 1], s_relall[:, 64 * i:64 * i + 64], axis=AX))
    b.wait("tensor", s_dve, s_dve.n)
    b.wait("tensor", s_misc, idf_ld)
    b.emit("tensor", lambda e: e.transpose(
        p_all[0:nch0, 3584:3712], s_rs[:], s_idf[:]), inc=s_pe)
    b.wait("scalar", s_pe, s_pe.n)
    b.emit("scalar", lambda e: e.activation(
        s_hrow[:], p_all[0:nch0, 3584:3712], AF.Exp, scale=1.0 / 64.0))
    b.wait("sync", s_act, s_act.n)
    b.emit("sync", lambda e: e.dma_start(
        out=hap(hrevd, 0, [(128, 44), (1, 128)]), in_=s_hrow[:]), inc=s_st, k=16)
    if dbg:
        b.emit("sync", lambda e: e.dma_start(
            out=hap(dbg_out["d_hrev"], 0, [(128, 44), (1, 128)]), in_=s_hrow[:]),
            inc=s_st, k=16)
    hrev_st = s_st.n
    s_G = [pa(f"p0G{i}", [128, T], BF16) for i in range(2)]
    for kc in range(KC):
        bb = kc % 2
        qeng = "sync" if bb == 0 else "scalar"
        off = 128 * (31 - kc)
        b.wait(qeng, s_st, hrev_st)
        b.wait(qeng, s_pe, pe_mark.get(("p0G", kc - 2), 0))
        b.emit(qeng, lambda e, off=off, bb=bb: e.dma_start(
            out=s_G[bb][:], in_=hap(hrevd, off, [(1, 128), (1, T)])),
            inc=s_g[bb], k=16, raw=True)
        b.wait("tensor", s_g[bb], s_g[bb].n)
        b.wait("tensor", s_pool, pool_init)
        b.wait("tensor", s_dve, pe_mark.get(("p0ebv", kc - 1), 0))
        for h in range(2):
            b.emit("tensor", lambda e, bb=bb, h=h: e.matmul(
                p_all[0:128, 512 * (5 + h):512 * (5 + h) + 512], s_J[:],
                s_G[bb][:, 512 * h:512 * h + 512], start=True, stop=True), inc=s_pe)
        pe_mark[("p0G", kc)] = s_pe.n
        b.wait("vector", s_pe, s_pe.n)
        b.emit("vector", lambda e, kc=kc: e.tensor_copy(
            ebt[kc][:], p_all[0:128, 2560:3584]))
        pe_mark[("p0ebv", kc)] = s_dve.n
    if dbg:
        b.wait("sync", s_dve, s_dve.n)
        b.emit("sync", lambda e: e.dma_start(out=dbg_out["d_ebt"], in_=ebt[4][:]),
               inc=s_st, k=16)
    barrier()
    close_phase()

    def layer_norm(tag, src_dram, xb_tiles, s_sq, s_tmp):
        for c in range(NCH):
            xbuf = c % 3
            b.wait("sync", s_dve, pe_mark.get((tag, "cast", c - 3), 0))
            b.emit("sync", lambda e, c=c, t=s_x[xbuf]: e.dma_start(
                out=t[:], in_=src_dram[128 * c:128 * c + 128, :]),
                inc=s_xs[xbuf], k=16)
            b.wait("vector", s_xs[xbuf], s_xs[xbuf].n)
            b.emit("vector", lambda e, c=c, t=s_x[xbuf]: e.tensor_copy(
                xb_tiles[c][:], t[:]))
            pe_mark[(tag, "cast", c)] = s_dve.n
            sqb = c % 2
            b.wait("vector", s_pe, pe_mark.get((tag, "sqmm", c - 2), 0))
            b.emit("vector", lambda e, c=c, sqb=sqb: e.tensor_tensor(
                s_sq[sqb][:], xb_tiles[c][:], xb_tiles[c][:], ALU.mult))
            b.wait("tensor", s_dve, s_dve.n)
            for h in range(2):
                b.emit("tensor", lambda e, c=c, h=h: e.matmul(
                    p_all[0:1, 512 * h:512 * h + 512], s_ones128[:],
                    xb_tiles[c][:, 512 * h:512 * h + 512],
                    start=(c == 0), stop=(c == NCH - 1)), inc=s_pe)
            for h in range(2):
                b.emit("tensor", lambda e, h=h, sqb=sqb, c=c: e.matmul(
                    p_all[0:1, 1024 + 512 * h:1536 + 512 * h], s_ones128[:],
                    s_sq[sqb][:, 512 * h:512 * h + 512],
                    start=(c == 0), stop=(c == NCH - 1)), inc=s_pe)
            pe_mark[(tag, "sqmm", c)] = s_pe.n
        b.wait("vector", s_pe, s_pe.n)

        M = s_rows[0:1, 0:1024]
        E2 = s_rows[0:1, 1024:2048]
        T3 = s_rows[0:1, 2048:3072]
        V4 = s_rows[0:1, 3072:4096]

        b.emit("vector", lambda e: e.tensor_scalar(
            M, p_all[0:1, 0:1024], 1.0 / 2048.0, None, ALU.mult))
        b.emit("vector", lambda e: e.tensor_scalar(
            E2, p_all[0:1, 1024:2048], 1.0 / 2048.0, None, ALU.mult))
        b.emit("vector", lambda e: e.tensor_tensor(T3, M, M, ALU.mult))
        b.emit("vector", lambda e: e.tensor_tensor(V4, E2, T3, ALU.subtract))
        b.wait("scalar", s_dve, s_dve.n)

        b.emit("scalar", lambda e: e.activation(E2, V4, AF.Ln, bias=EPS, scale=1.0))
        b.emit("scalar", lambda e: e.activation(T3, E2, AF.Exp, scale=-0.5))
        b.wait("vector", s_act, s_act.n)
        b.emit("vector", lambda e: e.tensor_tensor(V4, M, T3, ALU.mult))
        b.wait("tensor", s_dve, s_dve.n)
        for h in range(2):
            b.emit("tensor", lambda e, h=h: e.matmul(
                p_all[0:128, 512 * h:512 * h + 512], s_onesF[0:1, :],
                s_rows[0:1, 2048 + 512 * h:2048 + 512 * h + 512],
                start=True, stop=True), inc=s_pe)
            b.emit("tensor", lambda e, h=h: e.matmul(
                p_all[0:128, 1024 + 512 * h:1536 + 512 * h], s_onesF[0:1, :],
                s_rows[0:1, 3072 + 512 * h:3072 + 512 * h + 512],
                start=True, stop=True), inc=s_pe)
        pe_exp = s_pe.n
        for c in range(NCH):
            tb = c % 2
            b.wait("vector", s_pe, pe_exp)
            b.emit("vector", lambda e, c=c, tb=tb: e.tensor_tensor(
                s_tmp[tb][:], xb_tiles[c][:], p_all[0:128, 1024:2048], ALU.subtract),
    )
            b.emit("vector", lambda e, c=c, tb=tb: e.tensor_tensor(
                lnT[c][:], s_tmp[tb][:], p_all[0:128, 0:1024], ALU.mult))
        return s_dve.n

    def ffn(tag, src_dram, dst_dram, wT_dram, apply_mark, fs, post_oc=None,
            w_guard=0, wf_guard=0, sync_w=False):
        s_lgT, s_wT8, s_wexp, s_Z = fs
        b.wait("tensor", s_dve, apply_mark)
        for c in range(NCH):
            b.emit("tensor", lambda e, c=c: e.matmul(
                p_all[0:128, 0:512], s_AT[c][:], lnT[c][:, 0:512],
                start=(c == 0), stop=(c == NCH - 1)), inc=s_pe)
            b.emit("tensor", lambda e, c=c: e.matmul(
                p_all[0:128, 512:1024], s_AT[c][:], lnT[c][:, 512:1024],
                start=(c == 0), stop=(c == NCH - 1)), inc=s_pe)
            b.emit("tensor", lambda e, c=c: e.matmul(
                p_all[0:8, 1024:1536], s_RT[c][:], lnT[c][:, 0:512],
                start=(c == 0), stop=(c == NCH - 1)), inc=s_pe)
            b.emit("tensor", lambda e, c=c: e.matmul(
                p_all[0:8, 1536:2048], s_RT[c][:], lnT[c][:, 512:1024],
                start=(c == 0), stop=(c == NCH - 1)), inc=s_pe)
        pe_lora = s_pe.n
        b.wait("vector", s_pe, pe_lora)
        b.emit("vector", lambda e: e.tensor_copy(s_lgT[:], p_all[0:8, 1024:2048]),
   )
        lgT_ev = s_dve.n
        for c in range(8):
            b.wait("tensor", s_dve, max(lgT_ev, pe_mark.get((tag, "lgdone", c - 4), 0)))
            b.emit("tensor", lambda e, c=c: e.transpose(
                p_all[0:128, 2048 + 8 * (c % 4):2048 + 8 * (c % 4) + 8],
                s_lgT[0:8, 128 * c:128 * c + 128], s_idf[0:8, 0:8]), inc=s_pe)
            b.wait("vector", s_pe, s_pe.n)
            col = 2048 + 8 * (c % 4)

            b.emit("vector", lambda e, col=col: e.tensor_copy(
                s_lg[:], p_all[0:128, col:col + 8]))
            b.emit("vector", lambda e: e.reduce_max(s_r8[:, 0:1], s_lg[:], axis=AX))
            b.emit("vector", lambda e: e.tensor_scalar(
                s_lgb[0][:], s_lg[:], s_r8[:, 0:1], None, ALU.is_ge))
            b.emit("vector", lambda e: e.tensor_scalar(
                s_lgb[0][:], s_lgb[0][:], BIG, None, ALU.mult))
            b.emit("vector", lambda e: e.tensor_tensor(
                s_lgb[1][:], s_lg[:], s_lgb[0][:], ALU.subtract))
            b.emit("vector", lambda e: e.reduce_max(s_r8[:, 1:2], s_lgb[1][:], axis=AX))
            b.emit("vector", lambda e: e.tensor_scalar(
                s_r8[:, 2:3], s_r8[:, 0:1], -1.0, None, ALU.mult))
            pe_mark[(tag, "lgdone", c)] = s_dve.n
            b.wait("scalar", s_dve, s_dve.n)
            b.emit("scalar", lambda e: e.activation(
                s_lgb[2][:], s_lg[:], AF.Exp, bias=s_r8[:, 2:3], scale=1.0))
            b.wait("vector", s_act, s_act.n)
            b.wait("vector", s_pe, pe_mark.get((tag, "wtp", c - 1), 0))

            b.emit("vector", lambda e: e.tensor_scalar(
                s_lgb[0][:], s_lg[:], s_r8[:, 1:2], None, ALU.is_ge))
            b.emit("vector", lambda e: e.tensor_tensor(
                s_lgb[1][:], s_lgb[2][:], s_lgb[0][:], ALU.mult))
            b.emit("vector", lambda e: e.reduce_sum(s_r8[:, 3:4], s_lgb[1][:], axis=AX))
            b.emit("vector", lambda e: e.reciprocal(s_r8[:, 4:5], s_r8[:, 3:4]))
            b.emit("vector", lambda e: e.tensor_scalar(
                s_w8[:], s_lgb[1][:], s_r8[:, 4:5], None, ALU.mult))
            if dbg and c == 0 and tag == "f1":
                b.wait("sync", s_dve, s_dve.n)
                b.emit("sync", lambda e: e.dma_start(out=dbg_out["d_lg"], in_=s_lg[:]),
                       inc=s_st, k=16)
                b.emit("gpsimd", lambda e: e.dma_start(out=dbg_out["d_w"], in_=s_w8[:]),
                       inc=s_st, k=16)
            b.wait("tensor", s_dve, s_dve.n)
            b.emit("tensor", lambda e: e.transpose(
                p_all[0:8, 2560:2624].bitcast(BF16), s_w8[:], s_idb[:]), inc=s_pe)
            pe_mark[(tag, "wtp", c)] = s_pe.n
            b.wait("vector", s_pe, s_pe.n)
            b.emit("vector", lambda e, c=c: e.tensor_copy(
                s_wT8[0:8, 128 * c:128 * c + 128], p_all[0:8, 2560:2624].bitcast(BF16)),
    )
        b.wait("tensor", s_dve, s_dve.n)
        for h in range(2):
            b.emit("tensor", lambda e, h=h: e.matmul(
                p_all[0:128, 3072 + 512 * h:3072 + 512 * h + 512], s_E8[:],
                s_wT8[0:8, 512 * h:512 * h + 512], start=True, stop=True), inc=s_pe)
        b.wait("vector", s_pe, s_pe.n)
        b.emit("vector", lambda e: e.tensor_copy(s_wexp[:], p_all[0:128, 3072:4096]),
   )
        b.emit("vector", lambda e: e.tensor_tensor(
            s_Z[:], p_all[0:128, 0:1024], s_wexp[:], ALU.mult))
        z_done = s_dve.n
        if dbg and tag == "f1":
            b.wait("sync", s_dve, z_done)
            b.emit("sync", lambda e: e.dma_start(out=dbg_out["d_z"], in_=s_Z[:]),
                   inc=s_st, k=16)
        for oc in range(NCH):
            wb = oc % 2
            if sync_w:
                b.wait("sync", s_act, max(wf_guard,
                                          pe_mark.get((tag, "wcast", oc - 2), 0)))
                b.emit("sync", lambda e, oc=oc, wf=s_wf32[wb]: e.dma_start(
                    out=wf[:],
                    in_=hap(wT_dram, 128 * oc,
                            [(D, 128), (128 * D, NCH), (1, 128)])),
                    inc=s_wh[wb], k=16)
                b.wait("scalar", s_wh[wb], s_wh[wb].n)
                b.wait("scalar", s_pe, max(w_guard,
                                           pe_mark.get((tag, "base", oc - 2), 0)))
                b.emit("scalar", lambda e, wb=wb, wf=s_wf32[wb]: e.activation(
                    s_wsl[wb][:], wf[:], AF.Copy, scale=0.5))
                pe_mark[(tag, "wcast", oc)] = s_act.n
                pair = oc % 3
                b.wait("tensor", s_act, pe_mark[(tag, "wcast", oc)])
                b.wait("tensor", s_dve, max(z_done,
                                            pe_mark.get((tag, "evac", oc - 3), 0)))
            else:
                b.wait("gpsimd", s_pe, max(w_guard,
                                            pe_mark.get((tag, "base", oc - 2), 0)))
                b.emit("gpsimd", lambda e, oc=oc, wb=wb: e.dma_start(
                    out=s_wsl[wb][:],
                    in_=hap(wT_dram, 128 * oc,
                            [(D, 128), (128 * D, NCH), (1, 128)])),
                    inc=s_w[wb], k=16)
                b.wait("vector", s_w[wb], s_w[wb].n)
                b.emit("vector", lambda e, wb=wb: e.tensor_scalar(
                    s_wsl[wb][:], s_wsl[wb][:], 0.5, None, ALU.mult))
                wready = s_dve.n
                pair = oc % 3
                b.wait("tensor", s_dve, max(wready, z_done,
                                            pe_mark.get((tag, "evac", oc - 3), 0)))
            for dc in range(NCH):
                for h in range(2):
                    b.emit("tensor", lambda e, wb=wb, dc=dc, h=h, pair=pair: e.matmul(
                        p_all[0:128, 1024 * pair + 512 * h:1024 * pair + 512 * h + 512],
                        s_wsl[wb][:, 128 * dc:128 * dc + 128],
                        lnT[dc][:, 512 * h:512 * h + 512],
                        start=(dc == 0), stop=False), inc=s_pe)
            for h in range(2):
                b.emit("tensor", lambda e, oc=oc, h=h, pair=pair: e.matmul(
                    p_all[0:128, 1024 * pair + 512 * h:1024 * pair + 512 * h + 512],
                    s_Bf[:, 128 * oc:128 * oc + 128],
                    s_Z[:, 512 * h:512 * h + 512], start=False, stop=True), inc=s_pe)
            pe_mark[(tag, "base", oc)] = s_pe.n
            xb2 = oc % 3
            xq = "gpsimd" if sync_w else "sync"
            xss = s_gx if sync_w else s_xs
            sso = s_gso if sync_w else s_so
            if oc == 0 and sync_w:
                for ss in sso:
                    b.wait(xq, ss, ss.n)
            b.wait(xq, s_dve, max(apply_mark,
                                  pe_mark.get((tag, "xload", oc - 3), 0)))
            b.emit(xq, lambda e, oc=oc, t=s_x[xb2]: e.dma_start(
                out=t[:], in_=src_dram[128 * oc:128 * oc + 128, :]),
                inc=xss[xb2], k=16)
            b.wait("vector", xss[xb2], xss[xb2].n)
            b.wait("vector", s_pe, s_pe.n)
            ob = oc % 2
            b.wait("vector", sso[ob], sso[ob].n)
            b.emit("vector", lambda e, pair=pair, ti=s_x[xb2], to=s_xo[ob]:
                   e.tensor_tensor(to[:], ti[:],
                                   p_all[0:128, 1024 * pair:1024 * pair + 1024],
                                   ALU.add))
            pe_mark[(tag, "evac", oc)] = s_dve.n
            pe_mark[(tag, "xload", oc)] = s_dve.n
            b.wait(xq, s_dve, s_dve.n)
            b.emit(xq, lambda e, oc=oc, t=s_xo[ob]: e.dma_start(
                out=dst_dram[128 * oc:128 * oc + 128, :], in_=t[:]),
                inc=sso[ob], k=16)
            if post_oc is not None:
                post_oc(oc, s_xo[ob])

    open_phase()
    s_x = [pa(f"sx1{i}", [128, T], F32) for i in range(3)]
    s_xo = [pa(f"sxo1{i}", [128, T], F32) for i in range(2)]
    xb1 = [pa(f"xb{i}", [128, T], BF16) for i in range(NCH)]
    sq1 = [pa(f"sq{i}", [128, T], BF16) for i in range(2)]
    tp1 = [pa(f"tmp{i}", [128, T], F32) for i in range(2)]
    fs1 = (pa("slgT1", [8, T], F32), pa("swT81", [8, T], BF16),
           pa("swexp1", [128, T], BF16), pa("sZ1", [128, T], BF16))
    am = layer_norm("l1", xin, xb1, sq1, tp1)

    def ln2_chunk(oc, xo_tile):
        b.emit("vector", lambda e, oc=oc, t=xo_tile: e.tensor_copy(
            xb1[oc][:], t[:]))
        pe_mark[("l2cast", oc)] = s_dve.n

    ffn("f1", xin, x1d, wfT, am, fs1, post_oc=ln2_chunk)
    for c in range(NCH):
        sqb = c % 2
        b.wait("vector", s_pe, pe_mark.get(("l2sq", c - 2), 0))
        b.emit("vector", lambda e, c=c, sqb=sqb: e.tensor_tensor(
            sq1[sqb][:], xb1[c][:], xb1[c][:], ALU.mult))
        b.wait("tensor", s_dve, s_dve.n)
        if c == 0:
            b.wait("tensor", s_dve, pe_mark[("f1", "evac", NCH - 1)])
        for h in range(2):
            b.emit("tensor", lambda e, c=c, h=h: e.matmul(
                p_all[0:1, 512 * h:512 * h + 512], s_ones128[:],
                xb1[c][:, 512 * h:512 * h + 512],
                start=(c == 0), stop=(c == NCH - 1)), inc=s_pe)
        for h in range(2):
            b.emit("tensor", lambda e, h=h, sqb=sqb, c=c: e.matmul(
                p_all[0:1, 1024 + 512 * h:1536 + 512 * h], s_ones128[:],
                sq1[sqb][:, 512 * h:512 * h + 512],
                start=(c == 0), stop=(c == NCH - 1)), inc=s_pe)
        pe_mark[("l2sq", c)] = s_pe.n
    b.wait("vector", s_pe, s_pe.n)
    M = s_rows[0:1, 0:1024]
    E2 = s_rows[0:1, 1024:2048]
    T3 = s_rows[0:1, 2048:3072]
    V4 = s_rows[0:1, 3072:4096]
    b.emit("vector", lambda e: e.tensor_scalar(
        M, p_all[0:1, 0:1024], 1.0 / 2048.0, None, ALU.mult))
    b.emit("vector", lambda e: e.tensor_scalar(
        E2, p_all[0:1, 1024:2048], 1.0 / 2048.0, None, ALU.mult))
    b.emit("vector", lambda e: e.tensor_tensor(T3, M, M, ALU.mult))
    b.emit("vector", lambda e: e.tensor_tensor(V4, E2, T3, ALU.subtract))
    b.wait("scalar", s_dve, s_dve.n)
    b.emit("scalar", lambda e: e.activation(E2, V4, AF.Ln, bias=EPS, scale=1.0))
    b.emit("scalar", lambda e: e.activation(T3, E2, AF.Exp, scale=-0.5))
    b.wait("vector", s_act, s_act.n)
    b.emit("vector", lambda e: e.tensor_tensor(V4, M, T3, ALU.mult))
    b.wait("tensor", s_dve, s_dve.n)
    for h in range(2):
        b.emit("tensor", lambda e, h=h: e.matmul(
            p_all[0:128, 512 * h:512 * h + 512], s_onesF[0:1, :],
            s_rows[0:1, 2048 + 512 * h:2048 + 512 * h + 512],
            start=True, stop=True), inc=s_pe)
        b.emit("tensor", lambda e, h=h: e.matmul(
            p_all[0:128, 1024 + 512 * h:1536 + 512 * h], s_onesF[0:1, :],
            s_rows[0:1, 3072 + 512 * h:3072 + 512 * h + 512],
            start=True, stop=True), inc=s_pe)
    pe_exp2 = s_pe.n
    for c in range(NCH):
        tb2 = c % 2
        b.wait("vector", s_pe, pe_exp2)
        b.emit("vector", lambda e, c=c, tb2=tb2: e.tensor_tensor(
            tp1[tb2][:], xb1[c][:], p_all[0:128, 1024:2048], ALU.subtract))
        b.emit("vector", lambda e, c=c, tb2=tb2: e.tensor_tensor(
            lnT[c][:], tp1[tb2][:], p_all[0:128, 0:1024], ALU.mult))
    am2 = s_dve.n
    barrier()
    if dbg:
        b.emit("gpsimd", lambda e: e.dma_start(out=dbg_out["d_x1"], in_=x1d),
               inc=s_st, k=16)
    close_phase()

    open_phase()
    if dbg:
        b.wait("sync", s_dve, s_dve.n)
        for c in range(NCH):
            b.emit("sync", lambda e, c=c: e.dma_start(
                out=dbg_out["d_ln2"][128 * c:128 * c + 128, :], in_=lnT[c][:]),
                inc=s_st, k=16)
    s_wq = [pa(f"wq{i}", [128, D], BF16) for i in range(2)]
    s_wk = [pa(f"wk{i}", [128, D], BF16) for i in range(2)]
    s_wv = [pa(f"wv{i}", [128, D], BF16) for i in range(2)]
    s_qsl = pa("qsl", [128, T], BF16)
    s_ksl = pa("ksl", [128, T], BF16)
    s_vft = pa("vft", [128, T], BF16)
    s_vtok = [pa(f"vtok{i}", [128, 2080], BF16) for i in range(8)]

    for i in range(8):
        b.emit("vector", lambda e, i=i: e.memset(s_vtok[i][:], 1.0))
    vtok_ones = s_dve.n
    b.wait("gpsimd", s_dve, am2)
    for hp in range(HP):
        wb = hp % 2
        b.wait("gpsimd", s_pe, pe_mark.get(("qkv", hp - 2), 0))
        for wdram, sbufs in ((wqT, s_wq), (wkT, s_wk), (wvT, s_wv)):
            b.emit("gpsimd", lambda e, wdram=wdram, sbufs=sbufs, wb=wb, hp=hp: e.dma_start(
                out=sbufs[wb][:],
                in_=hap(wdram, 128 * hp, [(D, 128), (128 * D, NCH), (1, 128)])),
                inc=s_qkvw[wb], k=16)
        b.wait("tensor", s_qkvw[wb], s_qkvw[wb].n)
        b.wait("tensor", s_dve, max(am2, pe_mark.get(("qkvev", hp - 1), 0)))
        for sl, bk in ((s_wq, 0), (s_wk, 2), (s_wv, 4)):
            for dc in range(NCH):
                for h in range(2):
                    b.emit("tensor", lambda e, sl=sl, wb=wb, dc=dc, h=h, bk=bk: e.matmul(
                        p_all[0:128, 512 * (bk + h):512 * (bk + h) + 512],
                        sl[wb][:, 128 * dc:128 * dc + 128],
                        lnT[dc][:, 512 * h:512 * h + 512],
                        start=(dc == 0), stop=(dc == NCH - 1)), inc=s_pe)
        pe_mark[("qkv", hp)] = s_pe.n
        b.wait("vector", s_pe, s_pe.n)
        b.wait("vector", s_qk, s_qk.n)
        b.emit("vector", lambda e: e.tensor_copy(s_qsl[:], p_all[0:128, 0:1024]))
        b.emit("vector", lambda e: e.tensor_copy(s_ksl[:], p_all[0:128, 1024:2048]),
   )
        b.emit("vector", lambda e: e.tensor_copy(s_vft[:], p_all[0:128, 2048:3072]),
   )
        ev = s_dve.n
        pe_mark[("qkvev", hp)] = ev
        b.wait("sync", s_dve, ev)
        b.emit("sync", lambda e, hp=hp: e.dma_start(
            out=qtd[128 * hp:128 * hp + 128, :], in_=s_qsl[:]), inc=s_qk, k=16)
        b.emit("sync", lambda e, hp=hp: e.dma_start(
            out=kbs[hp // 4].ap()[128 * (hp % 4):128 * (hp % 4) + 128, :],
            in_=s_ksl[:]), inc=s_qk, k=16)
        b.wait("tensor", s_dve, max(ev, vtok_ones))
        for tc in range(8):
            b.wait("tensor", s_dve, pe_mark.get(("vtp", hp, tc - 3), 0))
            q4 = tc % 4
            b.emit("tensor", lambda e, tc=tc, q4=q4: e.transpose(
                p_all[0:128, 3072 + 64 * q4:3072 + 64 * q4 + 64].bitcast(BF16),
                s_vft[:, 128 * tc:128 * tc + 128], s_idb[:]), inc=s_pe)
            b.wait("vector", s_pe, s_pe.n)
            dst = s_vtok[tc][:, 130 * hp:130 * hp + 130].rearrange(
                "p (h c) -> p h c", h=2, c=65)[:, :, 0:64]
            src = p_all[0:128, 3072 + 64 * q4:3072 + 64 * q4 + 64].bitcast(
                BF16).rearrange("p (h c) -> p h c", h=2, c=64)
            b.emit("vector", lambda e, dst=dst, src=src: e.tensor_copy(dst, src),
       )
            pe_mark[("vtp", hp, tc)] = s_dve.n
        pe_mark[("qkve", hp)] = s_dve.n
        if hp % 4 == 3:
            st8 = hp // 4
            b.wait("sync", s_dve, s_dve.n)
            for tc in range(8):
                b.emit("sync", lambda e, tc=tc, st8=st8: e.dma_start(
                    out=vbs[st8].ap()[128 * tc:128 * tc + 128, :],
                    in_=s_vtok[tc][:, 520 * st8:520 * st8 + 520]), inc=s_st, k=16)
            b.wait("gpsimd", s_st, s_st.n)
            b.wait("gpsimd", s_qk, s_qk.n)
            b.emit("gpsimd", lambda e, st8=st8: e.collective_compute(
                "AllGather", ALU.bypass, replica_groups=GROUPS,
                ins=[kbs[st8].ap().opt()], outs=[kgs[st8].ap().opt()]), inc=s_cc)
            b.emit("gpsimd", lambda e, st8=st8: e.collective_compute(
                "AllGather", ALU.bypass, replica_groups=GROUPS,
                ins=[vbs[st8].ap().opt()], outs=[vgs[st8].ap().opt()]), inc=s_cc)
    cc_done = s_cc.n
    if dbg:
        b.emit("gpsimd", lambda e: e.dma_start(out=dbg_out["d_qt"], in_=qtd),
               inc=s_st, k=16)
    barrier()
    close_phase()

    open_phase()
    s_KT = [pa(f"aKT{i}", [128, 4096], BF16) for i in range(2)]
    s_QTt = [pa(f"aQT{i}", [128, T], BF16) for i in range(2)]
    s_vaug = [pa(f"aVA{i}", [128, 4160], BF16) for i in range(2)]
    s_pt = [pa(f"aPT{i}", [128, T], BF16) for i in range(4)]
    s_peb = [pa(f"aPB{i}", [128, T], BF16) for i in range(4)]
    s_ctxf = [pa(f"aCF{i}", [65, T], F32) for i in range(2)]
    s_sr = pa("aSR", [1, 2048], F32)
    s_ctxn = pa("aCN", [64, T], BF16)
    def load_hp(hp, buf):
        st8 = hp // 4
        b.wait("sync", s_cc, 2 * (st8 + 1))
        b.emit("sync", lambda e, hp=hp, buf=buf, st8=st8: e.dma_start(
            out=s_KT[buf][:],
            in_=hap(kgs[st8].ap(), (128 * (hp % 4)) * T,
                    [(T, 128), (512 * T, 4), (1, T)])), inc=s_att[buf], k=16)
        b.emit("sync", lambda e, hp=hp, buf=buf: e.dma_start(
            out=s_QTt[buf][:], in_=qtd[128 * hp:128 * hp + 128, :]), inc=s_att[buf], k=16)
        b.emit("sync", lambda e, hp=hp, buf=buf, st8=st8: e.dma_start(
            out=s_vaug[buf][:],
            in_=hap(vgs[st8].ap(), 130 * (hp % 4),
                    [(520, 128), (128 * 520, KC), (1, 130)])), inc=s_att[buf], k=16)

    load_hp(0, 0)
    pending = None
    for hp in range(HP):
        buf = hp % 2
        hp_ld = s_att[buf].n
        if hp + 1 < HP:
            b.wait("sync", s_pe, pe_mark.get(("att", hp - 1), 0))
            load_hp(hp + 1, (hp + 1) % 2)
        b.wait("tensor", s_att[buf], hp_ld)

        def mul_mark(kc, qh):
            if kc < 0:
                return pe_mark.get(("attmul", hp - 1, KC + kc, qh), 0)
            return pe_mark[("attmul", hp, kc, qh)]

        def exp_mark(kc, qh):
            return pe_mark.get(("attexp", hp, kc, qh),
                               pe_mark.get(("attexp", hp - 1, KC - 1, qh), 0))

        def emit_ctx(kc):
            for qh in range(2):
                b.wait("tensor", s_dve, mul_mark(kc, qh))
                if kc == 0:
                    b.wait("tensor", s_dve, pe_mark.get(("attev", hp - 1), 0))
                for hh in range(2):
                    b.emit("tensor", lambda e, kc=kc, qh=qh, hh=hh, buf=buf: e.matmul(
                        p_all[0:65, 512 * (4 + 2 * hh + qh):512 * (4 + 2 * hh + qh) + 512],
                        s_vaug[buf][:, 130 * kc + 65 * hh:130 * kc + 65 * hh + 65],
                        s_peb[2 * (kc % 2) + qh][:, 512 * hh:512 * hh + 512],
                        start=(kc == 0), stop=(kc == KC - 1)), inc=s_pe)
                pe_mark[("attctx", hp, kc, qh)] = s_pe.n

        for kc in range(KC):
            for qh in range(2):
                b.wait("tensor", s_act, exp_mark(kc - 1, qh))
                for hh in range(2):
                    b.emit("tensor", lambda e, kc=kc, qh=qh, hh=hh, buf=buf: e.matmul(
                        p_all[0:128, 1024 * qh + 512 * hh:1024 * qh + 512 * hh + 512],
                        s_KT[buf][64 * hh:64 * hh + 64, 128 * kc:128 * kc + 128],
                        s_QTt[buf][64 * hh:64 * hh + 64, 512 * qh:512 * qh + 512],
                        start=True, stop=True, tile_position=(64 * hh, 0)), inc=s_pe)
                pe_mark[("attsmm", hp, kc, qh)] = s_pe.n
            if kc == 0 and pending is not None:
                pending()
                pending = None
            if kc >= 1:
                emit_ctx(kc - 1)
            for qh in range(2):
                ptb = 2 * (kc % 2) + qh
                b.wait("scalar", s_pe, pe_mark[("attsmm", hp, kc, qh)])
                b.wait("scalar", s_dve, mul_mark(kc - 2, qh))
                b.emit("scalar", lambda e, qh=qh, ptb=ptb: e.activation(
                    s_pt[ptb][:], p_all[0:128, 1024 * qh:1024 * qh + 1024], AF.Exp,
                    bias=-SHIFT, scale=0.125))
                pe_mark[("attexp", hp, kc, qh)] = s_act.n
            for qh in range(2):
                ptb = 2 * (kc % 2) + qh
                b.wait("vector", s_act, pe_mark[("attexp", hp, kc, qh)])
                b.wait("vector", s_pe,
                       pe_mark.get(("attctx", hp, kc - 2, qh),
                                   pe_mark.get(("att", hp - 1), 0)))
                rep = hap(ebt[kc].ap(), 512 * qh, [(T, 128), (0, 2), (1, 512)])
                pbb = 2 * (kc % 2) + qh
                b.emit("vector", lambda e, pbb=pbb, ptb=ptb, rep=rep: e.tensor_tensor(
                    s_peb[pbb][:], s_pt[ptb][:], rep, ALU.mult))
                pe_mark[("attmul", hp, kc, qh)] = s_dve.n
        emit_ctx(KC - 1)
        pe_mark[("att", hp)] = s_pe.n
        b.wait("vector", s_pe, pe_mark[("att", hp)])
        b.wait("vector", s_act, pe_mark.get(("attrc", hp - 1), 0))
        for hh in range(2):
            b.emit("vector", lambda e, hh=hh: e.tensor_copy(
                s_sr[0:1, 1024 * hh:1024 * hh + 1024],
                p_all[64:65, 512 * (4 + 2 * hh):512 * (4 + 2 * hh) + 1024]))
        rows_ev = s_dve.n
        for hh in range(2):
            b.emit("vector", lambda e, hh=hh: e.tensor_copy(
                s_ctxf[hh][:, 0:512],
                p_all[0:65, 512 * (4 + 2 * hh):512 * (4 + 2 * hh) + 512]))
            b.emit("vector", lambda e, hh=hh: e.tensor_copy(
                s_ctxf[hh][:, 512:1024],
                p_all[0:65, 512 * (5 + 2 * hh):512 * (5 + 2 * hh) + 512]))
        pe_mark[("attevac", hp)] = s_dve.n
        b.wait("scalar", s_dve, rows_ev)
        b.wait("scalar", s_pe, pe_mark.get(("attxp", hp - 1), 0))
        b.emit("scalar", lambda e: e.activation(
            s_rows[0:1, 1024:2048], s_sr[0:1, 0:1024], AF.Ln))
        b.emit("scalar", lambda e: e.activation(
            s_rows[0:1, 0:1024], s_rows[0:1, 1024:2048], AF.Exp, scale=-1.0))
        b.emit("scalar", lambda e: e.activation(
            s_rows[0:1, 2048:3072], s_sr[0:1, 1024:2048], AF.Ln))
        b.emit("scalar", lambda e: e.activation(
            s_rows[0:1, 3072:4096], s_rows[0:1, 2048:3072], AF.Exp, scale=-1.0))
        pe_mark[("attrc", hp)] = s_act.n

        def flush_B(hp=hp):
            b.wait("tensor", s_act, pe_mark[("attrc", hp)])
            b.wait("tensor", s_dve, pe_mark[("attevac", hp)])
            for hh in range(2):
                base = 0 if hh == 0 else 3072
                for h2 in range(2):
                    b.emit("tensor", lambda e, h2=h2, hh=hh, base=base: e.matmul(
                        p_all[0:64, 512 * (4 + 2 * hh) + 512 * h2:
                              512 * (4 + 2 * hh) + 512 * h2 + 512],
                        s_onesF[0:1, 0:64],
                        s_rows[0:1, base + 512 * h2:base + 512 * h2 + 512],
                        start=True, stop=True), inc=s_pe)
            pe_mark[("attxp", hp)] = exp_mk = s_pe.n
            for hh in range(2):
                b.wait("vector", s_pe, exp_mk)
                b.wait("vector", s_ctxst, s_ctxst.n)
                b.emit("vector", lambda e, hh=hh: e.tensor_tensor(
                    s_ctxn[:], s_ctxf[hh][0:64, :],
                    p_all[0:64, 512 * (4 + 2 * hh):512 * (4 + 2 * hh) + 1024],
                    ALU.mult))
                b.wait("sync", s_dve, s_dve.n)
                b.emit("sync", lambda e, hp=hp, hh=hh: e.dma_start(
                    out=ctxd[128 * hp + 64 * hh:128 * hp + 64 * hh + 64, :],
                    in_=s_ctxn[:]), inc=s_ctxst, k=16)
            pe_mark[("attev", hp)] = s_dve.n
        pending = flush_B
    pending()
    pending = None
    if dbg:
        b.wait("gpsimd", s_st, s_st.n)
        b.emit("gpsimd", lambda e: e.dma_start(out=dbg_out["d_ctx"], in_=ctxd),
               inc=s_st, k=16)
    barrier()
    close_phase()
    ebt_scope.close()

    open_phase()
    s_x = [pa(f"sx4{i}", [128, T], F32) for i in range(3)]
    s_xo = [pa(f"sxo4{i}", [128, T], F32) for i in range(2)]
    s_wf32 = [pa(f"wf4{i}", [128, D], F32) for i in range(2)]
    xb3 = [pa(f"xd{i}", [128, T], BF16) for i in range(NCH)]
    ctx_scope = contextlib.ExitStack()
    s_ctxall = [ctx_scope.enter_context(nc.sbuf_tensor(f"oc{i}", [128, T], BF16))
                for i in range(NCH)]
    for c in range(NCH):
        b.emit("gpsimd", lambda e, c=c: e.dma_start(
            out=s_ctxall[c][:], in_=ctxd[128 * c:128 * c + 128, :]), inc=s_gld, k=16)
    ctx_ld = s_gld.n
    for oc in range(NCH):
        wb = oc % 2
        b.wait("sync", s_act, pe_mark.get(("opjcast", oc - 2), 0))
        b.emit("sync", lambda e, oc=oc, wf=s_wf32[wb]: e.dma_start(
            out=wf[:],
            in_=hap(woT, 128 * oc, [(D, 128), (128 * D, NCH), (1, 128)])),
            inc=s_wh[wb], k=16)
        b.wait("scalar", s_wh[wb], s_wh[wb].n)
        b.wait("scalar", s_pe, pe_mark.get(("opj", oc - 2), 0))
        b.emit("scalar", lambda e, wb=wb, wf=s_wf32[wb]: e.activation(
            s_wsl[wb][:], wf[:], AF.Copy, scale=1.0))
        pe_mark[("opjcast", oc)] = s_act.n
        b.wait("tensor", s_act, pe_mark[("opjcast", oc)])
        b.wait("tensor", s_gld, ctx_ld)
        pair = oc % 3
        b.wait("tensor", s_dve, pe_mark.get(("opje", oc - 3), 0))
        for dc in range(NCH):
            for h in range(2):
                b.emit("tensor", lambda e, wb=wb, dc=dc, h=h, pair=pair: e.matmul(
                    p_all[0:128, 1024 * pair + 512 * h:1024 * pair + 512 * h + 512],
                    s_wsl[wb][:, 128 * dc:128 * dc + 128],
                    s_ctxall[dc][:, 512 * h:512 * h + 512],
                    start=(dc == 0), stop=(dc == NCH - 1)), inc=s_pe)
        pe_mark[("opj", oc)] = s_pe.n
        xb2 = oc % 3
        b.wait("gpsimd", s_dve, pe_mark.get(("opjx", oc - 3), 0))
        b.emit("gpsimd", lambda e, oc=oc, t=s_x[xb2]: e.dma_start(
            out=t[:], in_=x1d[128 * oc:128 * oc + 128, :]), inc=s_gx[xb2], k=16)
        b.wait("vector", s_gx[xb2], s_gx[xb2].n)
        b.wait("vector", s_pe, s_pe.n)
        ob = oc % 2
        b.wait("vector", s_gso[ob], s_gso[ob].n)
        b.emit("vector", lambda e, pair=pair, ti=s_x[xb2], to=s_xo[ob]:
               e.tensor_tensor(to[:], ti[:],
                               p_all[0:128, 1024 * pair:1024 * pair + 1024],
                               ALU.add))
        pe_mark[("opje", oc)] = s_dve.n
        pe_mark[("opjx", oc)] = s_dve.n
        b.wait("gpsimd", s_dve, s_dve.n)
        b.emit("gpsimd", lambda e, oc=oc, t=s_xo[ob]: e.dma_start(
            out=x2d[128 * oc:128 * oc + 128, :], in_=t[:]), inc=s_gso[ob], k=16)
        b.emit("vector", lambda e, oc=oc, t=s_xo[ob]: e.tensor_copy(
            xb3[oc][:], t[:]))
    if dbg:
        b.wait("gpsimd", s_st, s_st.n)
        b.emit("gpsimd", lambda e: e.dma_start(out=dbg_out["d_x2"], in_=x2d),
               inc=s_st, k=16)
    ctx_scope.close()
    sq3 = [pa(f"sq3{i}", [128, T], BF16) for i in range(2)]
    tp3 = [pa("tmp3", [128, T], F32)]
    fs3 = (pa("slgT3", [8, T], F32), pa("swT83", [8, T], BF16),
           pa("swexp3", [128, T], BF16), pa("sZ3", [128, T], BF16))
    opj_all = pe_mark[("opj", NCH - 1)]
    for c in range(NCH):
        sqb = c % 2
        b.wait("vector", s_pe, max(opj_all, pe_mark.get(("l3sq", c - 2), 0)))
        b.emit("vector", lambda e, c=c, sqb=sqb: e.tensor_tensor(
            sq3[sqb][:], xb3[c][:], xb3[c][:], ALU.mult))
        b.wait("tensor", s_dve, s_dve.n)
        if c == 0:
            b.wait("tensor", s_dve, pe_mark[("opje", NCH - 1)])
        for h in range(2):
            b.emit("tensor", lambda e, c=c, h=h: e.matmul(
                p_all[0:1, 512 * h:512 * h + 512], s_ones128[:],
                xb3[c][:, 512 * h:512 * h + 512],
                start=(c == 0), stop=(c == NCH - 1)), inc=s_pe)
        for h in range(2):
            b.emit("tensor", lambda e, h=h, sqb=sqb, c=c: e.matmul(
                p_all[0:1, 1024 + 512 * h:1536 + 512 * h], s_ones128[:],
                sq3[sqb][:, 512 * h:512 * h + 512],
                start=(c == 0), stop=(c == NCH - 1)), inc=s_pe)
        pe_mark[("l3sq", c)] = s_pe.n
    b.wait("vector", s_pe, s_pe.n)
    M = s_rows[0:1, 0:1024]
    E2 = s_rows[0:1, 1024:2048]
    T3 = s_rows[0:1, 2048:3072]
    V4 = s_rows[0:1, 3072:4096]
    b.emit("vector", lambda e: e.tensor_scalar(
        M, p_all[0:1, 0:1024], 1.0 / 2048.0, None, ALU.mult))
    b.emit("vector", lambda e: e.tensor_scalar(
        E2, p_all[0:1, 1024:2048], 1.0 / 2048.0, None, ALU.mult))
    b.emit("vector", lambda e: e.tensor_tensor(T3, M, M, ALU.mult))
    b.emit("vector", lambda e: e.tensor_tensor(V4, E2, T3, ALU.subtract))
    b.wait("scalar", s_dve, s_dve.n)
    b.emit("scalar", lambda e: e.activation(E2, V4, AF.Ln, bias=EPS, scale=1.0))
    b.emit("scalar", lambda e: e.activation(T3, E2, AF.Exp, scale=-0.5))
    b.wait("vector", s_act, s_act.n)
    b.emit("vector", lambda e: e.tensor_tensor(V4, M, T3, ALU.mult))
    b.wait("tensor", s_dve, s_dve.n)
    for h in range(2):
        b.emit("tensor", lambda e, h=h: e.matmul(
            p_all[0:128, 512 * h:512 * h + 512], s_onesF[0:1, :],
            s_rows[0:1, 2048 + 512 * h:2048 + 512 * h + 512],
            start=True, stop=True), inc=s_pe)
        b.emit("tensor", lambda e, h=h: e.matmul(
            p_all[0:128, 1024 + 512 * h:1536 + 512 * h], s_onesF[0:1, :],
            s_rows[0:1, 3072 + 512 * h:3072 + 512 * h + 512],
            start=True, stop=True), inc=s_pe)
    pe_exp3 = s_pe.n
    for c in range(NCH):
        b.wait("vector", s_pe, pe_exp3)
        b.emit("vector", lambda e, c=c: e.tensor_tensor(
            tp3[0][:], xb3[c][:], p_all[0:128, 1024:2048], ALU.subtract))
        b.emit("vector", lambda e, c=c: e.tensor_tensor(
            lnT[c][:], tp3[0][:], p_all[0:128, 0:1024], ALU.mult))
    am3 = s_dve.n
    ffn("f2", x2d, xout, wfT, am3, fs3, w_guard=opj_all,
        wf_guard=pe_mark[("opjcast", NCH - 1)], sync_w=True)
    close_phase()

    for ss in (s_st, s_so[0], s_so[1]):
        b.wait("sync", ss, ss.n)
    for ss in (s_gso[0], s_gso[1]):
        b.wait("gpsimd", ss, ss.n)

    with nc.Block() as block:
        @block.sync
        def _(e):
            for f in b.streams["sync"]:
                f(e)

        @block.tensor
        def _(e):
            for f in b.streams["tensor"]:
                f(e)

        @block.vector
        def _(e):
            for f in b.streams["vector"]:
                f(e)

        @block.scalar
        def _(e):
            for f in b.streams["scalar"]:
                f(e)

        @block.gpsimd
        def _(e):
            for f in b.streams["gpsimd"]:
                f(e)

    ctx0.close()
    return nc


def make_inputs(full, core):
    bidx, s = core // 4, core % 4
    qb = 1024 * s
    x = np.asarray(full["L0_attn_hidden_states"])
    xT = np.ascontiguousarray(x[bidx, qb:qb + 1024, :].T)
    rel = np.asarray(full["L0_attn_rel_pos"])
    idx = np.clip(4606 - qb - np.arange(HREV), 0, 1022)
    relw = np.ascontiguousarray(rel[idx])
    E8 = np.zeros((8, 128), dtype=np.float32)
    for e in range(8):
        E8[e, 16 * e:16 * e + 16] = 1.0
    return dict(
        xin=xT,
        wqT=np.ascontiguousarray(np.asarray(full["L0_attn_wq"]).T),
        wkT=np.ascontiguousarray(np.asarray(full["L0_attn_wk"]).T),
        wvT=np.ascontiguousarray(np.asarray(full["L0_attn_wv"]).T),
        woT=np.ascontiguousarray(np.asarray(full["L0_attn_wo"]).T),
        wfT=np.ascontiguousarray(np.asarray(full["L0_ffn_weight"]).T),
        atT=np.ascontiguousarray(np.asarray(full["L0_ffn_lora_A"]).reshape(128, D).T),
        bff=np.ascontiguousarray(
            np.asarray(full["L0_ffn_lora_B"]).transpose(0, 2, 1).reshape(128, D)),
        rtT=np.ascontiguousarray(np.asarray(full["L0_ffn_router_w"]).T),
        relw=relw,
        jmat=np.eye(128, dtype=np.float32)[::-1].copy(),
        idm=np.eye(128, dtype=np.float32),
        e8m=E8,
    )


def kernel(**inputs):
    nc = build_nc(dbg=False)
    in_maps = [make_inputs(inputs, c) for c in range(8)]
    res = run_bass_kernel_spmd(nc, in_maps, core_ids=list(range(8)))
    out = np.empty((2, 4096, 2048), dtype=np.float32)
    for c in range(8):
        bidx, s = c // 4, c % 4
        out[bidx, 1024 * s:1024 * s + 1024, :] = np.asarray(res.results[c]["xout"]).T
    return out


if __name__ == "__main__":
    import jax
    cpu = jax.devices("cpu")[0]
    with jax.default_device(cpu):
        import reference as R
        ins = R.setup_inputs(0)
        ins = {k: np.asarray(v) for k, v in ins.items()}
        exp = np.asarray(R.reference(**ins))
    np.save("_dev_exp.npy", exp)
    got = kernel(**ins)
    num = float(np.mean((got - exp) ** 2))
    den = float(np.mean(exp ** 2))
    print("res-var ratio:", num / den)
